# version 05 (final)
# baseline (speedup 1.0000x reference)
import contextlib
import numpy as np
import ml_dtypes
import concourse.bass as bass
import concourse.mybir as mybir
from concourse.bass_utils import run_bass_kernel_spmd

F32 = mybir.dt.float32
BF16 = mybir.dt.bfloat16
AF = mybir.ActivationFunctionType
OP = mybir.AluOpType

DM = 2048
NH = 16
DH = 128
BATCH = 2
SEQ = 4096
CH = 1024
WIN = 3 * CH
PRE = 2 * CH
BLK = 256
HF = 8192
NC = 8
EPS_LN = 1e-5
EPS_A = 1e-6
FG = 512
CAPS = float(1.0 / (50.0 * np.sqrt(128.0)))
FW_SCALE = float(1.0 / np.sqrt(8192.0))

STOP = "full"

BF = ml_dtypes.bfloat16


def _win_masks():
    ktiles, apply_kb, base = [], [], []
    for qb in range(2):
        ks, ap, bm = [], [], {}
        ql = np.arange(qb * 512, qb * 512 + 512)
        for kb in range(24):
            kl = np.arange(kb * 128, kb * 128 + 128)
            valid = (kl[:, None] >= ql[None, :]) & (kl[:, None] <= ql[None, :] + 2048)
            if not valid.any():
                continue
            ks.append(kb)
            if kb < 8 or kb >= 16:
                ap.append(kb)
                bm[kb] = valid.astype(np.float32)
        ktiles.append(ks)
        apply_kb.append(ap)
        base.append(bm)
    return ktiles, apply_kb, base


_KTILES, _APPLY_KB, _BASEMASK = _win_masks()
N_KT = len(_KTILES[0])
N_APPLY = len(_APPLY_KB[0])
assert len(_KTILES[1]) == N_KT and len(_APPLY_KB[1]) == N_APPLY


def _rope_tables():
    half = DH // 2
    inv_freq = 1.0 / 10000.0 ** (np.arange(0, half, dtype=np.float64) / half)
    ang = np.outer(np.arange(SEQ, dtype=np.float64), inv_freq)
    ang = np.concatenate([ang, ang], axis=-1)
    return np.cos(ang).astype(np.float32), np.sin(ang).astype(np.float32)


def _hadamard(n):
    h = np.array([[1.0]], dtype=np.float32)
    while h.shape[0] < n:
        h = np.block([[h, h], [h, -h]])
    return h


def shard_inputs(d):
    x = np.asarray(d["L0_attn_a_hidden_states"], dtype=np.float32)
    cos_t, sin_t = _rope_tables()
    sinF = sin_t.copy()
    sinF[:, :DH // 2] = -sin_t[:, :DH // 2]

    def pmaj(wT):
        rows, O = wT.shape
        i = rows // 128
        a = wT.reshape(i, 128, O // 128, 128)
        return np.ascontiguousarray(a.transpose(1, 2, 0, 3).reshape(128, (O // 128) * i * 128))

    def pmaj512(wT):
        rows, O = wT.shape
        i = rows // 128
        a = wT.reshape(i, 128, O // 512, 512)
        return np.ascontiguousarray(a.transpose(1, 2, 0, 3).reshape(128, (O // 512) * i * 512))

    wqkvT = np.ascontiguousarray(np.asarray(d["L0_attn_a_w_qkv"], dtype=np.float32).T)
    wqkvP = pmaj(wqkvT[:, :2 * DM])
    wqkvV = pmaj512(wqkvT[:, 2 * DM:])
    wgP = pmaj(np.asarray(d["L0_attn_a_w_gate"], dtype=np.float32).T)
    woP = pmaj(np.asarray(d["L0_attn_a_w_out"], dtype=np.float32).T)
    w1P = pmaj(np.asarray(d["L0_ffn_W1"], dtype=np.float32).T)
    w3P = pmaj(np.asarray(d["L0_ffn_W3"], dtype=np.float32).T)
    w2T = np.ascontiguousarray(np.asarray(d["L0_ffn_W2"], dtype=np.float32).T)
    wbq = np.ascontiguousarray(np.asarray(d["L0_attn_b_wq"], dtype=np.float32))
    wbk = np.ascontiguousarray(np.asarray(d["L0_attn_b_wk"], dtype=np.float32))
    wbv = np.ascontiguousarray(np.asarray(d["L0_attn_b_wv"], dtype=np.float32))
    wboP = pmaj(np.asarray(d["L0_attn_b_wo"], dtype=np.float32))
    slope = np.asarray(d["L0_attn_a_slope_rate"], dtype=np.float32)
    normw = np.asarray(d["L0_attn_a_norm_w"], dtype=np.float32)
    qsc = np.asarray(d["L0_attn_b_query_scale"], dtype=np.float32)
    ksc = np.asarray(d["L0_attn_b_key_scale"], dtype=np.float32)

    idn = np.eye(128, dtype=np.float32).astype(BF)
    h128 = _hadamard(128).astype(BF)
    h64 = _hadamard(64).astype(BF)
    onesb = np.ones((128, 128), np.float32).astype(BF)
    ones1x128 = np.ones((1, 128), np.float32)
    iotaqd = np.broadcast_to(np.arange(1, BLK + 1, dtype=np.float32), (128, 256)).copy()
    p = np.arange(128, dtype=np.float32)
    iotakd = np.stack([255.0 - p, 127.0 - p], axis=1).copy()
    m = np.arange(256, dtype=np.float32)
    iotadT = np.ascontiguousarray(np.stack(
        [m[None, :] - p[:, None], m[None, :] - (p + 128.0)[:, None]], axis=1))
    tri = (iotadT >= 0).astype(np.float32)

    maps = []
    for core in range(NC):
        b, c = divmod(core, 4)
        seqx = x[b]

        def chunk(i):
            if 0 <= i < 4:
                return seqx[i * CH:(i + 1) * CH]
            return np.zeros((CH, DM), np.float32)

        xw_fm = np.ascontiguousarray(
            np.concatenate([chunk(c - 1), chunk(c), chunk(c + 1)], axis=0).T)
        xp_fm = np.ascontiguousarray(
            np.concatenate([chunk(c - 3), chunk(c - 2)], axis=0).T)

        w0 = (c - 1) * CH
        rc = np.zeros((WIN, DH), np.float32)
        rs = np.zeros((WIN, DH), np.float32)
        lo, hi = max(0, -w0), min(WIN, SEQ - w0)
        if hi > lo:
            rc[lo:hi] = cos_t[w0 + lo: w0 + hi]
            rs[lo:hi] = sinF[w0 + lo: w0 + hi]

        valid_chunks = [0 <= (c - 1 + j) < 4 for j in range(3)]
        bm = np.zeros((2, N_APPLY, 128, 512), np.float32)
        for qb in range(2):
            for i, kb in enumerate(_APPLY_KB[qb]):
                v = _BASEMASK[qb][kb].copy()
                if not valid_chunks[kb // 8]:
                    v[:] = 0.0
                bm[qb, i] = v

        maps.append({
            "xw_fm": xw_fm, "xp_fm": xp_fm,
            "wqkvP": wqkvP, "wqkvV": wqkvV, "wgP": wgP, "woP": woP,
            "w1P": w1P, "w3P": w3P, "w2T": w2T,
            "wbq": wbq, "wbk": wbk, "wbv": wbv, "wboP": wboP,
            "slope_r": slope.reshape(1, 16).copy(),
            "normw_pc": np.ascontiguousarray(normw.reshape(16, 128).T),
            "normw_row": normw.reshape(1, 2048).copy(),
            "qscale_rep": np.broadcast_to(qsc, (128, 128)).copy(),
            "kscale_rep": np.broadcast_to(ksc, (128, 128)).copy(),
            "ropecos": rc, "ropesinF": rs,
            "bmask": bm.astype(BF),
            "idn": idn, "h128": h128, "h64": h64, "onesb": onesb,
            "ones1x128": ones1x128,
            "iotaqd": iotaqd, "iotakd": iotakd,
            "iotadT": iotadT, "tri": tri,
        })
    return maps


ENGS = ("sync", "gpsimd", "tensor", "vector", "scalar")
T, V_, S, G, Y = "tensor", "vector", "scalar", "gpsimd", "sync"


class Emit:
    def __init__(self, nc, who, eng, sems):
        self.nc = nc
        self.who = who
        self.eng = eng
        self.sems = sems
        self.cnt = {k: 0 for k in sems}
        self.bank = {}

    def bump(self, sem, n):
        self.cnt[sem] += n
        return self.cnt[sem]

    def dma(self, q, out, in_, sem):
        self.bump(sem, 16)
        if self.who == q:
            self.eng.dma_start(out=out, in_=in_).then_inc(self.sems[sem], 16)

    def wait(self, who, sem, val=None):
        v = self.cnt[sem] if val is None else val
        if v <= 0:
            return
        if self.who == who:
            self.eng.wait_ge(self.sems[sem], v)

    def op(self, who, fn, sem=None, n=1):
        if sem is not None:
            self.bump(sem, n)
        if self.who == who:
            inst = fn(self.eng)
            if sem is not None:
                inst.then_inc(self.sems[sem], n)

    def drain(self, who):
        if self.who == who:
            self.eng.drain()

    def acq(self, bankname):
        st = self.bank.get(bankname)
        if st is not None:
            self.wait(T, st[0], st[1])

    def rel(self, bankname, sem):
        self.bank[bankname] = (sem, self.cnt[sem])

    def barrier(self):
        for s in ("s_pe", "s_v", "s_a", "s_sp", "s_out"):
            for w in ENGS:
                self.wait(w, s)


def build_nc(stop=STOP):
    nc = bass.Bass("TRN2", target_bir_lowering=False)
    D = {}

    def din(name, shape, dt=F32):
        D[name] = nc.dram_tensor(name, shape, dt, kind="ExternalInput").ap()

    def dint(name, shape, dt):
        D[name] = nc.dram_tensor(name, shape, dt, kind="Internal").ap()

    din("xw_fm", [DM, WIN]); din("xp_fm", [DM, PRE])
    din("wqkvP", [128, 32 * 2048]); din("wqkvV", [128, 16 * 2048])
    din("wgP", [128, 16 * 2048]); din("woP", [128, 16 * 2048])
    din("w1P", [128, 64 * 2048]); din("w3P", [128, 64 * 2048]); din("w2T", [HF, DM])
    din("wbq", [DM, DM]); din("wbk", [DM, 512]); din("wbv", [DM, 512]); din("wboP", [128, 16 * 2048])
    din("slope_r", [1, 16]); din("normw_pc", [128, 16]); din("normw_row", [1, DM])
    din("qscale_rep", [128, 128]); din("kscale_rep", [128, 128])
    din("ropecos", [WIN, DH]); din("ropesinF", [WIN, DH])
    din("bmask", [2, N_APPLY, 128, 512], BF16)
    din("idn", [128, 128], BF16); din("h128", [128, 128], BF16); din("h64", [64, 64], BF16)
    din("onesb", [128, 128], BF16); din("ones1x128", [1, 128])
    din("iotaqd", [128, 256]); din("iotakd", [128, 2])
    din("iotadT", [128, 2, 256]); din("tri", [128, 2, 256])

    dint("wqkvP_b", [128, 32 * 2048], BF16); dint("wqkvV_b", [128, 16 * 2048], BF16)
    dint("wgP_b", [128, 16 * 2048], BF16); dint("woP_b", [128, 16 * 2048], BF16)
    dint("w1P_b", [128, 64 * 2048], BF16); dint("w3P_b", [128, 64 * 2048], BF16)
    dint("at_b", [HF, DM], BF16); dint("w2ht_b", [128, 4 * 64 * 512], BF16)
    dint("x1d", [DM, WIN], F32); dint("x2d", [DM, WIN], F32); dint("x3d", [DM, CH], F32)
    dint("xn2d", [DM, WIN], BF16)
    dint("qtmd", [CH, DM], BF16)
    outy = nc.dram_tensor("outy", [DM, CH], F32, kind="ExternalOutput").ap()

    sem_names = ["s_in", "s_st", "s_ws", "s_sp", "s_pe", "s_v", "s_a", "s_out",
                 "s_x", "s_fx", "s_rp", "s_wv", "s_qt",
                 "s_bw0", "s_bw1", "sp0", "sp1", "sp2",
                 "pA0", "pA1", "pA2", "pB0", "pB1", "pC0", "pC1", "pC2",
                 "pG0", "pG1", "pG2"]

    with contextlib.ExitStack() as stk:
        def sb(name, shape, dt):
            return stk.enter_context(nc.sbuf_tensor(name, shape, dt))

        cidn = sb("cidn", [128, 128], BF16)
        ch128 = sb("ch128", [128, 128], BF16)
        ch64 = sb("ch64", [64, 64], BF16)
        conesb = sb("conesb", [128, 128], BF16)
        cones1 = sb("cones1", [1, 128], F32)
        cnormrow = sb("cnormrow", [1, DM], BF16)
        rstdb = sb("rstdb", [1, 512], BF16)
        cqsc = sb("cqsc", [128, 128], F32)
        cksc = sb("cksc", [128, 128], F32)
        ciotakd = sb("ciotakd", [128, 2], F32)
        cslope = sb("cslope", [1, 16], F32)
        nsrep = sb("nsrep", [128, 16], F32)
        bdrep = sb("bdrep", [128, 16], F32)
        kdect = sb("kdect", [128, NH, 2], F32)
        xf32 = sb("xf32", [128, 8192], F32)
        lnbf = sb("lnbf", [128, 2, 512], BF16)
        lnsq = sb("lnsq", [128, 2, 512], BF16)
        strow = sb("strow", [1, 2048], F32)
        repst = sb("repst", [128, 2, 512], F32)
        ftmp = sb("ftmp", [128, 2, 512], F32)
        fx2 = sb("fx2", [128, 1, 512], F32)
        wbufA = sb("wbufA", [128, 3, 2048], BF16)
        wbufB = sb("wbufB", [128, 2, 2048], BF16)
        silb = sb("silb", [128, 3, 512], BF16)
        ropeb = sb("ropeb", [128, 2, 2, 128], BF16)
        inv1 = sb("inv1", [1, 512], F32)
        arena = sb("arena", [128, 61440], BF16)
        invr = arena[:, 60416:61440].bitcast(F32)

        psn = [stk.enter_context(nc.psum_tensor(f"ps{i}", [128, 512], F32))
               for i in range(8)]
        sems = {s: stk.enter_context(nc.semaphore(s)) for s in sem_names}
        block = stk.enter_context(nc.Block())

        def av(off, n):
            return arena[:, off:off + n]

        XN = [av(0 + k * 512, 512) for k in range(16)]
        QF = [av(8192 + k * 512, 512) for k in range(16)]
        KF = [av(16384 + k * 512, 512) for k in range(16)]
        VT = [av(24576 + t * 2048, 2048) for t in range(4)]
        AO = [av(32768 + k * 512, 512) for k in range(16)]
        QKD = [[av(40960 + (p_ * 2 + s_) * 256, 256) for s_ in range(2)] for p_ in range(2)]
        QD2 = [av(41984 + p_ * 256, 256) for p_ in range(2)]
        KD2 = [[av(42496 + (p_ * 2 + s_) * 128, 128) for s_ in range(2)] for p_ in range(2)]
        GT = QF
        PR = KF
        XN2O = QF
        HID = [av(0 + h * 512, 512) for h in range(64)]
        XNF = [av(32768 + k * 512, 512) for k in range(16)]
        KFM = [av(0 + h * 3072, 3072) for h in range(4)]
        VTM = [av(12288 + t * 512, 512) for t in range(24)]
        XN3 = [av(24576 + k * 512, 512) for k in range(16)]
        KTM = av(32768, 512)
        KNR = av(33280, 512)
        QTM4 = [av(33792 + t * 2048, 2048) for t in range(4)]
        RSC1 = av(41984, 2048)
        RSC2 = av(44032, 2048)
        WVB = av(46080, 8192)
        QFMq = [av(33792 + h * 512, 512) for h in range(16)]
        EXPB = [av(41984 + i * 512, 512) for i in range(4)]
        AN = [av(44032 + k * 512, 512) for k in range(16)]
        BM = [av(52224 + i * 512, 512) for i in range(N_APPLY)]
        QTMR = av(58368, 2048)
        dmaskt = av(43008, 8192).rearrange("p (h s m) -> p h s m", s=2, m=256)
        qdect = av(51200, 4096).rearrange("p (h m) -> p h m", m=256)
        kvf = arena[:, 55296:55296 + 4096].bitcast(F32).rearrange("p (h e) -> p h e", e=128)
        kvb = av(59392, 2048).rearrange("p (h e) -> p h e", e=128)
        ciotaqd = arena[:, 55296:55296 + 512].bitcast(F32)
        ciotadT = arena[:, 55808:55808 + 1024].bitcast(F32).rearrange("p (s m) -> p s m", m=256)
        ctri = arena[:, 56832:56832 + 1024].bitcast(F32).rearrange("p (s m) -> p s m", m=256)

        XH = [xf32[:, k * 512:(k + 1) * 512] for k in range(16)]

        first_counts = {}

        def program(E: Emit):
            ps = psn

            E.dma(Y, cidn[:], D["idn"][:, :], "s_in")
            E.dma(Y, ch128[:], D["h128"][:, :], "s_in")
            E.dma(Y, ch64[:], D["h64"][:, :], "s_in")
            E.dma(Y, conesb[:], D["onesb"][:, :], "s_in")
            E.dma(Y, cones1[:], D["ones1x128"][:, :], "s_in")
            E.dma(G, cnormrow[:], D["normw_row"][:, :], "s_st")
            E.dma(Y, cqsc[:], D["qscale_rep"][:, :], "s_in")
            E.dma(Y, cksc[:], D["kscale_rep"][:, :], "s_in")
            E.dma(Y, ciotaqd[:], D["iotaqd"][:, :], "s_in")
            E.dma(Y, ciotakd[:], D["iotakd"][:, :], "s_in")
            E.dma(Y, ciotadT[:], D["iotadT"][:, :, :], "s_in")
            E.dma(Y, ctri[:], D["tri"][:, :, :], "s_in")
            E.dma(Y, cslope[:], D["slope_r"][:, :], "s_in")
            E.wait(T, "s_in")
            E.op(T, lambda t: t.matmul(ps[0][:, 0:16], cones1[:], cslope[:],
                                       start=True, stop=True), "s_pe")
            E.wait(V_, "s_pe")
            E.op(V_, lambda v: v.tensor_scalar_mul(nsrep[:], ps[0][:, 0:16], -1.0), "s_v")
            E.rel("ps0", "s_v")
            E.wait(S, "s_v")
            E.op(S, lambda s: s.activation(bdrep[:], nsrep[:], AF.Exp, scale=256.0), "s_a")
            for h in range(NH):
                E.op(S, lambda s, h=h: s.activation(
                    qdect[:, h, :], ciotaqd[:], AF.Exp, scale=nsrep[:, h:h + 1]), "s_a")
                E.op(S, lambda s, h=h: s.activation(
                    kdect[:, h, :], ciotakd[:], AF.Exp, scale=nsrep[:, h:h + 1]), "s_a")
                for su in range(2):
                    E.drain(S)
                    E.op(S, lambda s, h=h, su=su: s.activation(
                        ftmp[:, su, 0:256], ciotadT[:, su, :], AF.Exp,
                        scale=nsrep[:, h:h + 1]), "s_a")
                    E.wait(V_, "s_a")
                    E.drain(V_)
                    E.op(V_, lambda v, h=h, su=su: v.tensor_mul(
                        dmaskt[:, h, su, :], ftmp[:, su, 0:256], ctri[:, su, :]), "s_v")
                    E.wait(S, "s_v")
            E.drain(V_)
            E.op(V_, lambda v: v.memset(kvf[:], 0.0), "s_v")
            E.wait(S, "s_v")
            E.op(S, lambda s: s.copy(kvb[:], kvf[:]), "s_a")

            a_marks = []
            for dblk in range(4):
                for cg in range(64):
                    it = dblk * 64 + cg
                    rot = it % 3
                    qq = G
                    buf = wbufA[:, rot, 0:512]
                    if it >= 3:
                        E.wait(qq, "s_pe", a_marks[it - 3])
                    E.dma(qq, buf, D["w2T"][cg * 128:(cg + 1) * 128,
                                            dblk * 512:(dblk + 1) * 512], f"pG{rot}")
                    E.wait(T, f"pG{rot}")
                    E.acq(f"ps{rot}")
                    E.op(T, lambda t, rot=rot, buf=buf: t.matmul(
                        ps[rot][:], ch128[:], buf, start=True, stop=True), "s_pe")
                    a_marks.append(E.cnt["s_pe"])
                    E.wait(S, "s_pe")
                    E.wait(S, f"sp{rot}")
                    E.drain(S)
                    E.op(S, lambda s, rot=rot: s.activation(
                        silb[:, rot, :], ps[rot][:], AF.Copy), "s_a")
                    E.rel(f"ps{rot}", "s_a")
                    E.wait(Y, "s_a")
                    E.dma(Y, D["at_b"][cg * 128:(cg + 1) * 128,
                                       dblk * 512:(dblk + 1) * 512],
                          silb[:, rot, :], f"sp{rot}")
            passa_sp = [E.cnt["sp0"], E.cnt["sp1"], E.cnt["sp2"]]
            atr = D["at_b"].rearrange("(cg p) d -> cg p d", p=128)
            for w in (Y, G, S):
                for si in range(3):
                    E.wait(w, f"sp{si}", passa_sp[si])
            w2out = D["w2ht_b"].rearrange("p (db hh d) -> p db hh d", db=4, d=512)
            PB = [[av((par * 4 + db) * 512, 512) for db in range(4)] for par in range(2)]
            b_marks = []
            for rj in range(128):
                par = rj % 2
                qq = (G, Y)[par]
                buf = wbufB[:, par, :][0:64, :]
                if rj >= 2:
                    E.wait(qq, "s_pe", b_marks[4 * (rj - 2) + 3])
                E.dma(qq, buf, atr[:, rj, :], f"pA{par}")
                E.wait(T, f"pA{par}")
                for db in range(4):
                    E.acq(f"ps{par * 4 + db}")
                    E.op(T, lambda t, par=par, db=db, buf=buf: t.matmul(
                        ps[par * 4 + db][0:64, :], ch64[:],
                        buf[:, db * 512:(db + 1) * 512], start=True, stop=True), "s_pe")
                    b_marks.append(E.cnt["s_pe"])
                E.wait(S, "s_pe")
                E.wait(S, f"s_bw{par}")
                for db in range(4):
                    E.drain(S)
                    E.op(S, lambda s, par=par, db=db: s.activation(
                        PB[par][db][0:64, :], ps[par * 4 + db][0:64, :], AF.Copy,
                        scale=FW_SCALE), "s_a")
                    E.rel(f"ps{par * 4 + db}", "s_a")
                E.wait(S, "s_a")
                for db in range(4):
                    E.dma(S, w2out[rj, db, :, :], PB[par][db][0:64, :], f"s_bw{par}")
            stage_mark = {}
            for nm, bnm, ncols in (("wqkvP", "wqkvP_b", 32 * 2048),
                                   ("wqkvV", "wqkvV_b", 16 * 2048),
                                   ("wgP", "wgP_b", 16 * 2048),
                                   ("woP", "woP_b", 16 * 2048),
                                   ("w1P", "w1P_b", 64 * 2048),
                                   ("w3P", "w3P_b", 64 * 2048)):
                nsl = max(4, ncols // 16384)
                for sl in range(nsl):
                    c0, c1 = sl * (ncols // nsl), (sl + 1) * (ncols // nsl)
                    E.dma(G, D[bnm][:, c0:c1], D[nm][:, c0:c1], "s_st")
                stage_mark[nm] = E.cnt["s_st"]
            stage_done_w = E.cnt["s_st"]

            w2ht_bw0 = E.cnt["s_bw0"]
            w2ht_bw1 = E.cnt["s_bw1"]
            for w in ENGS:
                E.wait(w, "s_bw0", w2ht_bw0)
                E.wait(w, "s_bw1", w2ht_bw1)
            E.barrier()

            def ln_fm(x_tiles, xn_tiles, eps):
                E.wait(S, "s_v")
                pe_marks = []
                for k in range(16):
                    if k >= 2:
                        E.wait(V_, "s_pe", pe_marks[k - 2])
                        E.wait(S, "s_pe", pe_marks[k - 2])
                    E.drain(V_)
                    E.op(V_, lambda v, k=k: v.tensor_copy(lnbf[:, k % 2, :], x_tiles[k]), "s_v")
                    E.drain(S)
                    E.op(S, lambda s, k=k: s.activation(
                        lnsq[:, k % 2, :], x_tiles[k], AF.Square), "s_a")
                    E.wait(T, "s_v")
                    E.wait(T, "s_a")
                    if k == 0:
                        E.acq("ps6"); E.acq("ps7")
                    E.op(T, lambda t, k=k: t.matmul(
                        ps[6][0:1, :], conesb[:, 0:1], lnbf[:, k % 2, :],
                        start=(k == 0), stop=(k == 15)), "s_pe")
                    E.op(T, lambda t, k=k: t.matmul(
                        ps[7][0:1, :], conesb[:, 0:1], lnsq[:, k % 2, :],
                        start=(k == 0), stop=(k == 15)), "s_pe")
                    pe_marks.append(E.cnt["s_pe"])
                E.wait(V_, "s_pe")
                E.drain(V_)
                E.op(V_, lambda v: v.tensor_copy(strow[:, 0:512], ps[6][0:1, :]), "s_v")
                E.op(V_, lambda v: v.tensor_copy(strow[:, 512:1024], ps[7][0:1, :]), "s_v")
                E.rel("ps6", "s_v"); E.rel("ps7", "s_v")
                E.drain(V_)
                E.op(V_, lambda v: v.tensor_scalar_mul(strow[:, 0:512], strow[:, 0:512], 1.0 / DM), "s_v")
                E.op(V_, lambda v: v.tensor_scalar_mul(strow[:, 512:1024], strow[:, 512:1024], 1.0 / DM), "s_v")
                E.drain(V_)
                E.op(V_, lambda v: v.tensor_mul(strow[:, 1024:1536], strow[:, 0:512], strow[:, 0:512]), "s_v")
                E.drain(V_)
                E.op(V_, lambda v: v.tensor_sub(strow[:, 1024:1536], strow[:, 512:1024], strow[:, 1024:1536]), "s_v")
                E.drain(V_)
                E.op(V_, lambda v: v.tensor_scalar_add(strow[:, 1024:1536], strow[:, 1024:1536], float(eps)), "s_v")
                E.wait(S, "s_v")
                E.drain(S)
                E.op(S, lambda s: s.activation(strow[:, 1536:2048], strow[:, 1024:1536], AF.Sqrt), "s_a")
                E.wait(V_, "s_a")
                E.drain(V_)
                E.op(V_, lambda v: v.reciprocal(strow[:, 1024:1536], strow[:, 1536:2048]), "s_v")
                E.drain(V_)
                E.op(V_, lambda v: v.tensor_mul(strow[:, 512:1024], strow[:, 0:512], strow[:, 1024:1536]), "s_v")
                E.wait(T, "s_v")
                E.acq("ps6"); E.acq("ps7")
                E.op(T, lambda t: t.matmul(ps[6][:], cones1[:], strow[:, 1024:1536],
                                           start=True, stop=True), "s_pe")
                E.op(T, lambda t: t.matmul(ps[7][:], cones1[:], strow[:, 512:1024],
                                           start=True, stop=True), "s_pe")
                E.wait(V_, "s_pe")
                E.drain(V_)
                E.op(V_, lambda v: v.tensor_copy(repst[:, 0, :], ps[6][:]), "s_v")
                E.op(V_, lambda v: v.tensor_copy(repst[:, 1, :], ps[7][:]), "s_v")
                E.rel("ps6", "s_v"); E.rel("ps7", "s_v")
                for k in range(16):
                    E.drain(V_)
                    E.op(V_, lambda v, k=k: v.tensor_mul(ftmp[:, k % 2, :], x_tiles[k], repst[:, 0, :]), "s_v")
                    E.drain(V_)
                    E.op(V_, lambda v, k=k: v.tensor_sub(xn_tiles[k], ftmp[:, k % 2, :], repst[:, 1, :]), "s_v")

            def proj_c1(wdram, blk0, rhs_tiles, evac, st_mark=0):
                marks = []
                E.wait(Y, "s_pe")
                E.wait(Y, "s_st", st_mark)
                for oc in range(16):
                    rot = oc % 3
                    buf = wbufA[:, rot, :]
                    if oc >= 3:
                        E.wait(Y, "s_pe", marks[oc - 3])
                    j = blk0 + oc
                    E.dma(Y, buf, wdram[:, j * 2048:(j + 1) * 2048], f"pA{rot}")
                    E.wait(T, f"pA{rot}")
                    pb = f"ps{4 + rot}"
                    E.acq(pb)
                    for i in range(16):
                        E.op(T, lambda t, i=i, rot=rot, buf=buf: t.matmul(
                            ps[4 + rot][:],
                            buf.rearrange("p (i o) -> p i o", o=128)[:, i, :],
                            rhs_tiles[i], start=(i == 0), stop=(i == 15)),
                            "s_pe" if i == 15 else None)
                    marks.append(E.cnt["s_pe"])
                    evac(oc, 4 + rot)

            for si in range(5):
                prefix = si < 2
                for hf in range(2):
                    if si < 2:
                        base, coff = D["xp_fm"], si * CH + hf * 512
                    else:
                        base, coff = D["xw_fm"], (si - 2) * CH + hf * 512
                    wcol = (si - 2) * CH + hf * 512

                    E.barrier()
                    E.dma(Y, xf32[:].rearrange("p (k t) -> p k t", t=512),
                          base.rearrange("(k p) t -> p k t", p=128)[:, :, coff:coff + 512],
                          "s_x")
                    E.wait(V_, "s_x"); E.wait(S, "s_x"); E.wait(T, "s_x")
                    ln_fm(XH, XN, EPS_LN)
                    E.wait(T, "s_v")

                    def ev_k(oc, pi):
                        E.wait(S, "s_pe")
                        E.drain(S)
                        E.op(S, lambda s, oc=oc, pi=pi: s.activation(
                            KF[oc], ps[pi][:], AF.Silu), "s_a")
                        E.rel(f"ps{pi}", "s_a")
                    proj_c1(D["wqkvP_b"], 16, XN, ev_k, stage_mark["wqkvP"])

                    vpw = arena[:, 32768:40960].rearrange("p (i o) -> p i o", o=512)
                    E.wait(S, "s_st", stage_mark["wqkvV"])
                    for vc in range(4):
                        E.wait(S, "s_pe")
                        E.dma(S, vpw, D["wqkvV_b"][:, vc * 8192:(vc + 1) * 8192], "s_ws")
                        E.wait(T, "s_ws")
                        for tb in range(4):
                            pb = f"ps{4 + (tb % 2)}"
                            E.acq(pb)
                            for i in range(16):
                                E.op(T, lambda t, i=i, tb=tb: t.matmul(
                                    ps[4 + (tb % 2)][:],
                                    XN[i][:, tb * 128:(tb + 1) * 128],
                                    vpw[:, i, :], start=(i == 0), stop=(i == 15)),
                                    "s_pe" if i == 15 else None)
                            E.wait(S, "s_pe")
                            E.drain(S)
                            E.op(S, lambda s, tb=tb, vc=vc: s.activation(
                                VT[tb][:, vc * 512:(vc + 1) * 512],
                                ps[4 + (tb % 2)][:], AF.Silu), "s_a")
                            E.rel(f"ps{4 + (tb % 2)}", "s_a")

                    if not prefix:
                        def ev_q(oc, pi):
                            E.wait(S, "s_pe")
                            E.drain(S)
                            E.op(S, lambda s, oc=oc, pi=pi: s.activation(
                                QF[oc], ps[pi][:], AF.Silu), "s_a")
                            E.rel(f"ps{pi}", "s_a")
                        proj_c1(D["wqkvP_b"], 0, XN, ev_q, stage_mark["wqkvP"])

                    last_qkd_pe = [0, 0]
                    last_kd_pe = [0, 0]
                    for blk in range(2):
                        tb0 = blk * 2
                        E.wait(T, "s_a")
                        for h in range(NH):
                            par = h % 2
                            if not prefix:
                                for su in range(2):
                                    E.acq(f"ps{su}")
                                    E.op(T, lambda t, h=h, su=su, blk=blk: t.matmul(
                                        ps[su][:, 0:256],
                                        KF[h][:, blk * 256 + su * 128: blk * 256 + (su + 1) * 128],
                                        QF[h][:, blk * 256:(blk + 1) * 256],
                                        start=True, stop=True), "s_pe")
                                qk_pe = E.cnt["s_pe"]
                                E.wait(V_, "s_pe", max(qk_pe, last_qkd_pe[par]))
                                E.drain(V_)
                                for su in range(2):
                                    E.op(V_, lambda v, h=h, su=su, par=par: v.tensor_mul(
                                        QKD[par][su], ps[su][:, 0:256], dmaskt[:, h, su, :]), "s_v")
                                    E.rel(f"ps{su}", "s_v")
                                E.op(V_, lambda v, h=h, blk=blk, par=par: v.tensor_mul(
                                    QD2[par], QF[h][:, blk * 256:(blk + 1) * 256],
                                    qdect[:, h, :]), "s_v")
                                qkd_v = E.cnt["s_v"]
                            for su in range(2):
                                E.acq(f"ps{2 + su}")
                                E.op(T, lambda t, h=h, su=su, blk=blk: t.matmul(
                                    ps[2 + su][:, 0:128],
                                    KF[h][:, blk * 256 + su * 128: blk * 256 + (su + 1) * 128],
                                    cidn[:], start=True, stop=True), "s_pe")
                            kd_pe = E.cnt["s_pe"]
                            E.wait(S, "s_pe", max(kd_pe, last_kd_pe[par]))
                            for su in range(2):
                                E.drain(S)
                                E.op(S, lambda s, h=h, su=su, par=par: s.activation(
                                    KD2[par][su], ps[2 + su][:, 0:128], AF.Copy,
                                    scale=kdect[:, h, su:su + 1]), "s_a")
                                E.rel(f"ps{2 + su}", "s_a")
                            kd_a = E.cnt["s_a"]

                            if not prefix:
                                E.wait(T, "s_v", qkd_v)
                                E.acq("ps6")
                                for su in range(2):
                                    E.op(T, lambda t, h=h, su=su, tb0=tb0, par=par: t.matmul(
                                        ps[6][:, 0:256],
                                        VT[tb0 + su][:, h * 128:(h + 1) * 128],
                                        QKD[par][su], start=(su == 0), stop=False), None)
                                E.op(T, lambda t, h=h, par=par: t.matmul(
                                    ps[6][:, 0:256], kvb[:, h, :], QD2[par],
                                    start=False, stop=True), "s_pe")
                                last_qkd_pe[par] = E.cnt["s_pe"]
                                E.wait(S, "s_pe")
                                E.drain(S)
                                E.op(S, lambda s, h=h, blk=blk: s.activation(
                                    AO[h][:, blk * 256:(blk + 1) * 256],
                                    ps[6][:, 0:256], AF.Copy), "s_a")
                                E.rel("ps6", "s_a")

                            E.wait(T, "s_a", kd_a)
                            E.acq("ps7")
                            for su in range(2):
                                E.op(T, lambda t, h=h, su=su, tb0=tb0, par=par: t.matmul(
                                    ps[7][:, 0:128], KD2[par][su],
                                    VT[tb0 + su][:, h * 128:(h + 1) * 128],
                                    start=(su == 0), stop=(su == 1)),
                                    "s_pe" if su == 1 else None)
                            last_kd_pe[par] = E.cnt["s_pe"]
                            E.wait(V_, "s_pe")
                            E.drain(V_)
                            E.op(V_, lambda v, h=h: v.scalar_tensor_tensor(
                                kvf[:, h, :], kvf[:, h, :], bdrep[:, h:h + 1],
                                ps[7][:, 0:128], OP.mult, OP.add), "s_v")
                            E.rel("ps7", "s_v")
                            E.wait(S, "s_v")
                            E.drain(S)
                            E.op(S, lambda s, h=h: s.copy(kvb[:, h, :], kvf[:, h, :]), "s_a")

                    if prefix:
                        continue

                    def ev_g(oc, pi):
                        E.wait(S, "s_pe")
                        E.drain(S)
                        E.op(S, lambda s, oc=oc, pi=pi: s.activation(
                            GT[oc], ps[pi][:], AF.Sigmoid), "s_a")
                        E.rel(f"ps{pi}", "s_a")
                    proj_c1(D["wgP_b"], 0, XN, ev_g, stage_mark["wgP"])

                    pe_marks = []
                    for k in range(16):
                        if k >= 2:
                            E.wait(S, "s_pe", pe_marks[k - 2])
                        E.drain(S)
                        E.op(S, lambda s, k=k: s.activation(
                            lnsq[:, k % 2, :], AO[k], AF.Square), "s_a")
                        E.wait(T, "s_a")
                        if k == 0:
                            E.acq("ps6")
                        E.op(T, lambda t, k=k: t.matmul(
                            ps[6][0:1, :], conesb[:, 0:1], lnsq[:, k % 2, :],
                            start=(k == 0), stop=(k == 15)), "s_pe")
                        pe_marks.append(E.cnt["s_pe"])
                    E.wait(V_, "s_pe")
                    E.drain(V_)
                    E.op(V_, lambda v: v.tensor_scalar_mul(strow[:, 0:512], ps[6][0:1, :], 1.0 / DM), "s_v")
                    E.rel("ps6", "s_v")
                    E.drain(V_)
                    E.op(V_, lambda v: v.tensor_scalar_add(strow[:, 0:512], strow[:, 0:512], EPS_A), "s_v")
                    E.wait(S, "s_v")
                    E.drain(S)
                    E.op(S, lambda s: s.activation(strow[:, 512:1024], strow[:, 0:512], AF.Sqrt), "s_a")
                    E.wait(V_, "s_a")
                    E.drain(V_)
                    E.op(V_, lambda v: v.reciprocal(strow[:, 1024:1536], strow[:, 512:1024]), "s_v")
                    E.drain(V_)
                    E.op(V_, lambda v: v.tensor_copy(rstdb[:], strow[:, 1024:1536]), "s_v")
                    rms_v = E.cnt["s_v"]
                    E.wait(T, "s_v", rms_v)
                    E.wait(T, "s_st", 16)
                    for k in range(16):
                        pb = f"ps{4 + (k % 2)}"
                        E.acq(pb)
                        E.op(T, lambda t, k=k: t.matmul(
                            ps[4 + (k % 2)][:], cnormrow[:, k * 128:(k + 1) * 128],
                            rstdb[:], start=True, stop=True), "s_pe")
                        E.wait(V_, "s_pe")
                        E.drain(V_)
                        E.op(V_, lambda v, k=k: v.tensor_mul(
                            silb[:, k % 2, :], AO[k], ps[4 + (k % 2)][:]), "s_v")
                        E.rel(pb, "s_v")
                        E.drain(V_)
                        E.op(V_, lambda v, k=k: v.tensor_mul(PR[k], silb[:, k % 2, :], GT[k]), "s_v")
                    E.wait(T, "s_v")

                    def ev_o(oc, pi):
                        E.wait(V_, "s_pe")
                        E.drain(V_)
                        E.op(V_, lambda v, oc=oc, pi=pi: v.tensor_add(
                            XH[oc], XH[oc], ps[pi][:]), "s_v")
                        E.rel(f"ps{pi}", "s_v")
                    proj_c1(D["woP_b"], 0, PR, ev_o, stage_mark["woP"])
                    x1_v = E.cnt["s_v"]

                    E.wait(Y, "s_v", x1_v)
                    E.dma(Y, D["x1d"].rearrange("(k p) t -> p k t", p=128)[:, :, wcol:wcol + 512],
                          xf32[:].rearrange("p (k t) -> p k t", t=512), "s_sp")
                    ln_fm(XH, XN2O, EPS_LN)
                    E.wait(Y, "s_v")
                    E.dma(Y, D["xn2d"].rearrange("(k p) t -> p k t", p=128)[:, :, wcol:wcol + 512],
                          arena[:, 8192:16384].rearrange("p (k t) -> p k t", t=512), "s_sp")

            if stop == "x1":
                E.barrier()
                E.wait(Y, "s_sp")
                E.dma(Y, outy[:, :], D["x1d"][:, CH:2 * CH], "s_out")
                E.wait(Y, "s_out")
                return

            w1r = D["w1P_b"]
            w3r = D["w3P_b"]
            w2hr = D["w2ht_b"].rearrange("p (db hh d) -> p db hh d", db=4, d=512)

            def ffn_sweep(xn_tiles, res_dram, res_col, out_dram, out_col, store_sem):
                pe_w = []
                E.wait(G, "s_st", stage_mark["w1P"])
                E.wait(Y, "s_st", stage_mark["w1P"])
                E.wait(S, "s_st", stage_mark["w3P"])
                for hc in range(64):
                    rot = hc % 3
                    rotb = hc % 2
                    qq = (G, Y)[hc % 2]
                    bufa = wbufA[:, rot, :]
                    bufb = wbufB[:, rotb, :]
                    if hc >= 3:
                        E.wait(qq, "s_pe", pe_w[2 * (hc - 3)])
                    if hc >= 2:
                        E.wait(S, "s_pe", pe_w[2 * (hc - 2) + 1])
                    E.dma(qq, bufa, w1r[:, hc * 2048:(hc + 1) * 2048], f"pA{rot}")
                    E.dma(S, bufb, w3r[:, hc * 2048:(hc + 1) * 2048], f"pC{rotb}")
                    E.wait(T, f"pA{rot}"); E.wait(T, f"pC{rotb}")
                    pa, pb = f"ps{rot}", f"ps{3 + rotb}"
                    E.acq(pa); E.acq(pb)
                    for i in range(16):
                        E.op(T, lambda t, i=i, rot=rot, bufa=bufa: t.matmul(
                            ps[rot][:], bufa.rearrange("p (i o) -> p i o", o=128)[:, i, :],
                            xn_tiles[i], start=(i == 0), stop=(i == 15)),
                            "s_pe" if i == 15 else None)
                    pe_w.append(E.cnt["s_pe"])
                    for i in range(16):
                        E.op(T, lambda t, i=i, rotb=rotb, bufb=bufb: t.matmul(
                            ps[3 + rotb][:], bufb.rearrange("p (i o) -> p i o", o=128)[:, i, :],
                            xn_tiles[i], start=(i == 0), stop=(i == 15)),
                            "s_pe" if i == 15 else None)
                    pe_w.append(E.cnt["s_pe"])
                    E.wait(S, "s_pe", pe_w[-2])
                    E.wait(S, "s_v")
                    E.drain(S)
                    E.op(S, lambda s, rot=rot: s.activation(
                        silb[:, rot, :], ps[rot][:], AF.Silu), "s_a")
                    E.rel(pa, "s_a")
                    E.wait(V_, "s_a")
                    E.wait(V_, "s_pe", pe_w[-1])
                    E.drain(V_)
                    E.op(V_, lambda v, hc=hc, rot=rot, rotb=rotb: v.tensor_mul(
                        HID[hc], silb[:, rot, :], ps[3 + rotb][:]), "s_v")
                    E.rel(pb, "s_v")
                E.wait(T, "s_v")
                E.wait(Y, "s_pe")
                E.wait(G, "s_pe")
                h4_marks = []
                for dblk in range(4):
                    for pi in range(4):
                        E.acq(f"ps{pi}")
                    for h4 in range(16):
                        it4 = dblk * 16 + h4
                        rot = it4 % 3
                        qq2 = (Y, G)[it4 % 2]
                        buf = wbufA[:, rot, :]
                        if it4 >= 3:
                            E.wait(qq2, "s_pe", h4_marks[it4 - 3])
                        E.dma(qq2, buf[:, 0:2048].rearrange("p (hh d) -> p hh d", d=512),
                              w2hr[:, dblk, h4 * 4:(h4 + 1) * 4, :], f"pA{rot}")
                        E.wait(T, f"pA{rot}")
                        for hh in range(4):
                            h = h4 * 4 + hh
                            for dsub in range(4):
                                E.op(T, lambda t, h=h, hh=hh, dsub=dsub, buf=buf: t.matmul(
                                    ps[dsub][:],
                                    buf.rearrange("p (hh d) -> p hh d", d=512)[:, hh, dsub * 128:(dsub + 1) * 128],
                                    HID[h], start=(h == 0), stop=(h == 63)),
                                    "s_pe" if dsub == 3 else None)
                        h4_marks.append(E.cnt["s_pe"])
                    w2_pe = E.cnt["s_pe"]
                    for dsub in range(4):
                        d = dblk * 4 + dsub
                        E.wait(Y, "s_v")
                        E.dma(Y, fx2[:, 0, :],
                              res_dram[d * 128:(d + 1) * 128, res_col:res_col + 512], "s_fx")
                        E.wait(V_, "s_pe", w2_pe)
                        E.wait(V_, "s_fx")
                        E.wait(V_, store_sem)
                        E.drain(V_)
                        E.op(V_, lambda v, dsub=dsub: v.tensor_add(
                            ftmp[:, dsub % 2, :], ps[dsub][:], fx2[:, 0, :]), "s_v")
                        E.rel(f"ps{dsub}", "s_v")
                        E.wait(Y, "s_v")
                        E.dma(Y, out_dram[d * 128:(d + 1) * 128, out_col:out_col + 512],
                              ftmp[:, dsub % 2, :], store_sem)

            E.wait(Y, "s_bw0", w2ht_bw0)
            E.wait(Y, "s_bw1", w2ht_bw1)
            for g in range(WIN // FG):
                col = g * FG
                E.barrier()
                E.dma(Y, arena[:, 32768:40960].rearrange("p (k t) -> p k t", t=512),
                      D["xn2d"].rearrange("(k p) t -> p k t", p=128)[:, :, col:col + 512], "s_x")
                E.wait(T, "s_x"); E.wait(V_, "s_x"); E.wait(S, "s_x")
                ffn_sweep(XNF, D["x1d"], col, D["x2d"], col, "s_sp")

            if stop == "x2":
                E.barrier()
                E.wait(Y, "s_sp")
                E.dma(Y, outy[:, :], D["x2d"][:, CH:2 * CH], "s_out")
                E.wait(Y, "s_out")
                return

            def rope_norm(dst, heads, wt0, scale_rep):
                rot = (wt0 // 128) % 2
                E.wait(G, "s_v")
                E.dma(G, ropeb[:, rot, 0, :], D["ropecos"][wt0:wt0 + 128, :], "s_rp")
                E.dma(G, ropeb[:, rot, 1, :], D["ropesinF"][wt0:wt0 + 128, :], "s_rp")
                E.wait(V_, "s_rp")
                W = heads * 128
                dv = dst.rearrange("p (h j) -> p h j", j=128)
                cosv = ropeb[:, rot, 0, :].unsqueeze(1).broadcast_to([128, heads, 128])
                sinL = ropeb[:, rot, 1, 0:64].unsqueeze(1).broadcast_to([128, heads, 64])
                sinH = ropeb[:, rot, 1, 64:128].unsqueeze(1).broadcast_to([128, heads, 64])
                t1 = RSC1[:, 0:W].rearrange("p (h j) -> p h j", j=128)
                t2 = RSC2[:, 0:W].rearrange("p (h j) -> p h j", j=128)
                E.drain(V_)
                E.op(V_, lambda v: v.tensor_mul(t1[:, :, 0:64], dv[:, :, 64:128], sinL), "s_v")
                E.op(V_, lambda v: v.tensor_mul(t1[:, :, 64:128], dv[:, :, 0:64], sinH), "s_v")
                E.drain(V_)
                E.op(V_, lambda v: v.tensor_mul(t2, dv, cosv), "s_v")
                E.drain(V_)
                E.op(V_, lambda v: v.tensor_add(dv, t1, t2), "s_v")
                E.wait(S, "s_v")
                E.drain(S)
                E.op(S, lambda s: s.activation(RSC1[:, 0:W], dst, AF.Square), "s_a")
                E.wait(V_, "s_a")
                E.drain(V_)
                sq = RSC1[:, 0:W].rearrange("p (h j) -> p h j", j=128)
                E.op(V_, lambda v: v.reduce_sum(
                    ftmp[:, 0, 0:heads].rearrange("p (h o) -> p h o", o=1), sq,
                    axis=mybir.AxisListType.X), "s_v")
                E.wait(S, "s_v")
                E.drain(S)
                E.op(S, lambda s: s.activation(ftmp[:, 0, 16:16 + heads], ftmp[:, 0, 0:heads], AF.Sqrt), "s_a")
                E.wait(V_, "s_a")
                E.drain(V_)
                E.op(V_, lambda v: v.tensor_scalar_add(ftmp[:, 0, 32:32 + heads], ftmp[:, 0, 16:16 + heads], EPS_A), "s_v")
                E.drain(V_)
                E.op(V_, lambda v: v.reciprocal(ftmp[:, 0, 48:48 + heads], ftmp[:, 0, 32:32 + heads]), "s_v")
                E.drain(V_)
                invv = ftmp[:, 0, 48:48 + heads].rearrange("p (h o) -> p h o", o=1).broadcast_to([128, heads, 128])
                E.op(V_, lambda v: v.tensor_mul(dv, dv, invv), "s_v")
                E.drain(V_)
                scv = scale_rep[:, :].unsqueeze(1).broadcast_to([128, heads, 128])
                E.op(V_, lambda v: v.tensor_mul(dv, dv, scv), "s_v")

            wbkr = D["wbk"].rearrange("(i p) o -> p i o", p=128)
            wbvr = D["wbv"].rearrange("(i p) o -> p i o", p=128)
            wbqr = D["wbq"].rearrange("(i p) o -> p i o", p=128)
            wvbv = WVB.rearrange("p (i o) -> p i o", o=512)

            for g in range(6):
                col = g * 512
                E.barrier()
                E.wait(Y, "s_sp")
                E.dma(Y, xf32[:].rearrange("p (k t) -> p k t", t=512),
                      D["x2d"].rearrange("(k p) t -> p k t", p=128)[:, :, col:col + 512], "s_x")
                E.wait(V_, "s_x"); E.wait(S, "s_x"); E.wait(T, "s_x")
                ln_fm(XH, XN3, EPS_LN)
                E.wait(T, "s_v")

                E.wait(G, "s_pe")
                E.dma(G, wvbv, wbkr[:, :, :], "s_wv")
                E.wait(T, "s_wv")
                for tb in range(4):
                    pb = f"ps{4 + (tb % 2)}"
                    E.acq(pb)
                    for i in range(16):
                        E.op(T, lambda t, i=i, tb=tb: t.matmul(
                            ps[4 + (tb % 2)][:], XN3[i][:, tb * 128:(tb + 1) * 128],
                            wvbv[:, i, :], start=(i == 0), stop=(i == 15)),
                            "s_pe" if i == 15 else None)
                    E.wait(S, "s_pe")
                    E.drain(S)
                    E.op(S, lambda s: s.activation(KTM, ps[4 + (tb % 2)][:], AF.Copy), "s_a")
                    E.rel(pb, "s_a")
                    E.wait(V_, "s_a")
                    rope_norm(KTM, 4, col + tb * 128, cksc)
                    kn_v = E.cnt["s_v"]
                    E.wait(T, "s_v", kn_v)
                    for kvh in range(4):
                        pq = f"ps{kvh % 2}"
                        E.acq(pq)
                        E.op(T, lambda t, kvh=kvh: t.matmul(
                            ps[kvh % 2][:, 0:128], KTM[:, kvh * 128:(kvh + 1) * 128],
                            cidn[:], start=True, stop=True), "s_pe")
                        E.wait(S, "s_pe")
                        E.drain(S)
                        E.op(S, lambda s, kvh=kvh, g=g, tb=tb: s.activation(
                            KFM[kvh][:, col + tb * 128: col + tb * 128 + 128],
                            ps[kvh % 2][:, 0:128], AF.Copy), "s_a")
                        E.rel(pq, "s_a")

                E.wait(G, "s_a")
                E.dma(G, wvbv, wbvr[:, :, :], "s_wv")
                E.wait(T, "s_wv")
                for tb in range(4):
                    pb = f"ps{4 + (tb % 2)}"
                    E.acq(pb)
                    for i in range(16):
                        E.op(T, lambda t, i=i, tb=tb: t.matmul(
                            ps[4 + (tb % 2)][:], XN3[i][:, tb * 128:(tb + 1) * 128],
                            wvbv[:, i, :], start=(i == 0), stop=(i == 15)),
                            "s_pe" if i == 15 else None)
                    E.wait(S, "s_pe")
                    E.drain(S)
                    E.op(S, lambda s, g=g, tb=tb: s.activation(
                        VTM[g * 4 + tb], ps[4 + (tb % 2)][:], AF.Copy), "s_a")
                    E.rel(pb, "s_a")

                if g in (2, 3):
                    for cb in range(4):
                        E.wait(G, "s_a")
                        E.dma(G, wvbv, wbqr[:, :, cb * 512:(cb + 1) * 512], "s_wv")
                        E.wait(T, "s_wv")
                        for tb in range(4):
                            pb = f"ps{4 + (tb % 2)}"
                            E.acq(pb)
                            for i in range(16):
                                E.op(T, lambda t, i=i, tb=tb: t.matmul(
                                    ps[4 + (tb % 2)][:], XN3[i][:, tb * 128:(tb + 1) * 128],
                                    wvbv[:, i, :], start=(i == 0), stop=(i == 15)),
                                    "s_pe" if i == 15 else None)
                            E.wait(S, "s_pe")
                            E.drain(S)
                            E.op(S, lambda s, tb=tb, cb=cb: s.activation(
                                QTM4[tb][:, cb * 512:(cb + 1) * 512],
                                ps[4 + (tb % 2)][:], AF.Copy), "s_a")
                            E.rel(pb, "s_a")
                    for tb in range(4):
                        E.wait(V_, "s_a")
                        rope_norm(QTM4[tb], 16, col + tb * 128, cqsc)
                        E.wait(Y, "s_v")
                        E.dma(Y, D["qtmd"][(g - 2) * 512 + tb * 128:(g - 2) * 512 + (tb + 1) * 128, :],
                              QTM4[tb], "s_sp")

            if stop == "x3_pre":
                E.barrier()
                E.wait(Y, "s_sp")
                E.dma(Y, outy[:, :], D["x2d"][:, CH:2 * CH], "s_out")
                E.wait(Y, "s_out")
                return

            wbor = D["wboP"]
            for qb in range(2):
                E.barrier()
                E.wait(Y, "s_sp")
                E.dma(Y, arena[:, 52224:52224 + N_APPLY * 512].rearrange(
                    "p (a t) -> p a t", t=512),
                    D["bmask"][qb].rearrange("a p t -> p a t"), "s_x")
                for sub in range(4):
                    E.wait(Y, "s_a")
                    E.dma(Y, QTMR, D["qtmd"][qb * 512 + sub * 128: qb * 512 + (sub + 1) * 128, :], "s_qt")
                    E.wait(T, "s_qt")
                    for h in range(NH):
                        pq = f"ps{h % 2}"
                        E.acq(pq)
                        E.op(T, lambda t, h=h: t.matmul(
                            ps[h % 2][:, 0:128], QTMR[:, h * 128:(h + 1) * 128],
                            cidn[:], start=True, stop=True), "s_pe")
                        E.wait(S, "s_pe")
                        E.drain(S)
                        E.op(S, lambda s, h=h, sub=sub: s.activation(
                            QFMq[h][:, sub * 128:(sub + 1) * 128],
                            ps[h % 2][:, 0:128], AF.Copy), "s_a")
                        E.rel(pq, "s_a")
                E.wait(T, "s_a"); E.wait(V_, "s_x")

                for h in range(NH):
                    kvh = h // 4
                    apply_pos = {kb: i for i, kb in enumerate(_APPLY_KB[qb])}
                    E.acq("ps2"); E.acq("ps3")
                    exp_marks = [0, 0, 0, 0]
                    tiles = list(enumerate(_KTILES[qb]))
                    score_marks = {}

                    def emit_score(idx, kb):
                        rot = idx % 2
                        E.acq(f"ps{rot}")
                        E.op(T, lambda t, kb=kb, kvh=kvh, h=h, rot=rot: t.matmul(
                            ps[rot][:], KFM[kvh][:, kb * 128:(kb + 1) * 128],
                            QFMq[h], start=True, stop=True), "s_pe")
                        score_marks[idx] = E.cnt["s_pe"]

                    emit_score(0, tiles[0][1])
                    for idx, kb in tiles:
                        if idx + 1 < len(tiles):
                            emit_score(idx + 1, tiles[idx + 1][1])
                        rot = idx % 2
                        er = idx % 4
                        pq = f"ps{rot}"
                        E.wait(S, "s_pe", max(score_marks[idx], exp_marks[er]))
                        E.drain(S)
                        E.op(S, lambda s, rot=rot: s.activation(
                            ftmp[:, rot, :], ps[rot][:], AF.Tanh, scale=CAPS), "s_a")
                        E.rel(pq, "s_a")
                        E.drain(S)
                        E.op(S, lambda s, rot=rot, er=er: s.activation(
                            EXPB[er], ftmp[:, rot, :], AF.Exp, scale=50.0), "s_a")
                        exp_a = E.cnt["s_a"]
                        masked = kb in apply_pos
                        if masked:
                            E.wait(V_, "s_a", exp_a)
                            E.drain(V_)
                            E.op(V_, lambda v, er=er, kb=kb: v.tensor_mul(
                                EXPB[er], EXPB[er], BM[apply_pos[kb]]), "s_v")
                            E.wait(T, "s_v")
                        else:
                            E.wait(T, "s_a", exp_a)
                        first, last = idx == 0, idx == N_KT - 1
                        E.op(T, lambda t, er=er, first=first, last=last: t.matmul(
                            ps[3][0:1, :], conesb[:, 0:1], EXPB[er],
                            start=first, stop=last), None)
                        E.op(T, lambda t, er=er, kb=kb, kvh=kvh, first=first, last=last: t.matmul(
                            ps[2][:], VTM[kb][:, kvh * 128:(kvh + 1) * 128], EXPB[er],
                            start=first, stop=last),
                            "s_pe")
                        exp_marks[er] = E.cnt["s_pe"]
                    av_pe = E.cnt["s_pe"]
                    E.wait(V_, "s_pe", av_pe)
                    E.drain(V_)
                    E.op(V_, lambda v: v.reciprocal(inv1[:], ps[3][0:1, :]), "s_v")
                    E.rel("ps3", "s_v")
                    E.wait(T, "s_v")
                    E.acq("ps4")
                    E.op(T, lambda t: t.matmul(ps[4][:], cones1[:], inv1[:],
                                               start=True, stop=True), "s_pe")
                    E.wait(V_, "s_pe")
                    E.drain(V_)
                    E.op(V_, lambda v: v.tensor_copy(invr[:], ps[4][:]), "s_v")
                    E.rel("ps4", "s_v")
                    E.drain(V_)
                    E.op(V_, lambda v, h=h: v.tensor_mul(AN[h], ps[2][:], invr[:]), "s_v")
                    E.rel("ps2", "s_v")
                E.wait(T, "s_v")

                omarks = []
                E.wait(G, "s_pe")
                for oc in range(16):
                    rot = oc % 2
                    buf = wbufA[:, rot, :]
                    if oc >= 2:
                        E.wait(G, "s_pe", omarks[oc - 2])
                    E.dma(G, buf, wbor[:, oc * 2048:(oc + 1) * 2048], f"pG{rot}")
                    E.wait(T, f"pG{rot}")
                    pb = f"ps{4 + rot}"
                    E.acq(pb)
                    for i in range(16):
                        E.op(T, lambda t, i=i, rot=rot, buf=buf: t.matmul(
                            ps[4 + rot][:],
                            buf.rearrange("p (i o) -> p i o", o=128)[:, i, :],
                            AN[i], start=(i == 0), stop=(i == 15)),
                            "s_pe" if i == 15 else None)
                    omarks.append(E.cnt["s_pe"])
                    E.wait(Y, "s_v")
                    E.dma(Y, fx2[:, 0, :],
                          D["x2d"][oc * 128:(oc + 1) * 128, CH + qb * 512: CH + qb * 512 + 512],
                          "s_fx")
                    E.wait(V_, "s_pe")
                    E.wait(V_, "s_fx")
                    E.wait(V_, "s_sp")
                    E.drain(V_)
                    E.op(V_, lambda v, rot=rot: v.tensor_add(
                        ftmp[:, rot, :], ps[4 + rot][:], fx2[:, 0, :]), "s_v")
                    E.rel(pb, "s_v")
                    E.wait(Y, "s_v")
                    E.dma(Y, D["x3d"][oc * 128:(oc + 1) * 128, qb * 512: qb * 512 + 512],
                          ftmp[:, rot, :], "s_sp")

            if stop == "x3":
                E.barrier()
                E.wait(Y, "s_sp")
                E.dma(Y, outy[:, :], D["x3d"][:, :], "s_out")
                E.wait(Y, "s_out")
                return

            for g in range(CH // FG):
                col = g * FG
                E.barrier()
                E.wait(Y, "s_sp")
                E.dma(Y, xf32[:].rearrange("p (k t) -> p k t", t=512),
                      D["x3d"].rearrange("(k p) t -> p k t", p=128)[:, :, col:col + 512], "s_x")
                E.wait(V_, "s_x"); E.wait(S, "s_x"); E.wait(T, "s_x")
                ln_fm(XH, XNF, EPS_LN)
                E.wait(T, "s_v")
                ffn_sweep(XNF, D["x3d"], col, outy, col, "s_out")

            E.barrier()
            E.wait(Y, "s_out")

        def run_pass(who, eng):
            em = Emit(nc, who, eng, sems)
            program(em)
            if not first_counts:
                first_counts.update(em.cnt)
            else:
                assert em.cnt == first_counts, (who, em.cnt, first_counts)

        @block.sync
        def _(e):
            run_pass("sync", e)

        @block.gpsimd
        def _(e):
            run_pass("gpsimd", e)

        @block.tensor
        def _(e):
            run_pass("tensor", e)

        @block.vector
        def _(e):
            run_pass("vector", e)

        @block.scalar
        def _(e):
            run_pass("scalar", e)

    return nc


def kernel(**inputs):
    maps = shard_inputs(inputs)
    nc = build_nc()
    res = run_bass_kernel_spmd(nc, maps, core_ids=list(range(NC)))
    out = np.zeros((BATCH, SEQ, DM), np.float32)
    for core in range(NC):
        b, c = divmod(core, 4)
        out[b, c * CH:(c + 1) * CH] = np.asarray(res.results[core]["outy"]).T
    return out


# version 06
# speedup vs baseline: 1.0139x; 1.0139x over previous
import contextlib
import numpy as np
import ml_dtypes
import concourse.bass as bass
import concourse.mybir as mybir
from concourse.bass_utils import run_bass_kernel_spmd

F32 = mybir.dt.float32
BF16 = mybir.dt.bfloat16
AF = mybir.ActivationFunctionType
OP = mybir.AluOpType

DM = 2048
NH = 16
DH = 128
BATCH = 2
SEQ = 4096
CH = 1024
WIN = 3 * CH
PRE = 2 * CH
BLK = 256
HF = 8192
NC = 8
EPS_LN = 1e-5
EPS_A = 1e-6
FG = 512
CAPS = float(1.0 / (50.0 * np.sqrt(128.0)))
FW_SCALE = float(1.0 / np.sqrt(8192.0))

STOP = "full"

BF = ml_dtypes.bfloat16


def _win_masks():
    ktiles, apply_kb, base = [], [], []
    for qb in range(2):
        ks, ap, bm = [], [], {}
        ql = np.arange(qb * 512, qb * 512 + 512)
        for kb in range(24):
            kl = np.arange(kb * 128, kb * 128 + 128)
            valid = (kl[:, None] >= ql[None, :]) & (kl[:, None] <= ql[None, :] + 2048)
            if not valid.any():
                continue
            ks.append(kb)
            if kb < 8 or kb >= 16:
                ap.append(kb)
                bm[kb] = valid.astype(np.float32)
        ktiles.append(ks)
        apply_kb.append(ap)
        base.append(bm)
    return ktiles, apply_kb, base


_KTILES, _APPLY_KB, _BASEMASK = _win_masks()
N_KT = len(_KTILES[0])
N_APPLY = len(_APPLY_KB[0])
assert len(_KTILES[1]) == N_KT and len(_APPLY_KB[1]) == N_APPLY


def _rope_tables():
    half = DH // 2
    inv_freq = 1.0 / 10000.0 ** (np.arange(0, half, dtype=np.float64) / half)
    ang = np.outer(np.arange(SEQ, dtype=np.float64), inv_freq)
    ang = np.concatenate([ang, ang], axis=-1)
    return np.cos(ang).astype(np.float32), np.sin(ang).astype(np.float32)


def _hadamard(n):
    h = np.array([[1.0]], dtype=np.float32)
    while h.shape[0] < n:
        h = np.block([[h, h], [h, -h]])
    return h


def shard_inputs(d):
    x = np.asarray(d["L0_attn_a_hidden_states"], dtype=np.float32)
    cos_t, sin_t = _rope_tables()
    sinF = sin_t.copy()
    sinF[:, :DH // 2] = -sin_t[:, :DH // 2]

    def pmaj(wT):
        rows, O = wT.shape
        i = rows // 128
        a = wT.reshape(i, 128, O // 128, 128)
        return np.ascontiguousarray(a.transpose(1, 2, 0, 3).reshape(128, (O // 128) * i * 128))

    def pmaj512(wT):
        rows, O = wT.shape
        i = rows // 128
        a = wT.reshape(i, 128, O // 512, 512)
        return np.ascontiguousarray(a.transpose(1, 2, 0, 3).reshape(128, (O // 512) * i * 512))

    wqkvT = np.ascontiguousarray(np.asarray(d["L0_attn_a_w_qkv"], dtype=np.float32).T)
    wqkvP = pmaj(wqkvT[:, :2 * DM])
    wqkvV = pmaj512(wqkvT[:, 2 * DM:])
    wgP = pmaj(np.asarray(d["L0_attn_a_w_gate"], dtype=np.float32).T)
    woP = pmaj(np.asarray(d["L0_attn_a_w_out"], dtype=np.float32).T)
    w1P = pmaj(np.asarray(d["L0_ffn_W1"], dtype=np.float32).T)
    w3P = pmaj(np.asarray(d["L0_ffn_W3"], dtype=np.float32).T)
    w2T = np.ascontiguousarray(np.asarray(d["L0_ffn_W2"], dtype=np.float32).T)
    wbq = np.ascontiguousarray(np.asarray(d["L0_attn_b_wq"], dtype=np.float32))
    wbk = np.ascontiguousarray(np.asarray(d["L0_attn_b_wk"], dtype=np.float32))
    wbv = np.ascontiguousarray(np.asarray(d["L0_attn_b_wv"], dtype=np.float32))
    wboP = pmaj(np.asarray(d["L0_attn_b_wo"], dtype=np.float32))
    slope = np.asarray(d["L0_attn_a_slope_rate"], dtype=np.float32)
    normw = np.asarray(d["L0_attn_a_norm_w"], dtype=np.float32)
    qsc = np.asarray(d["L0_attn_b_query_scale"], dtype=np.float32)
    ksc = np.asarray(d["L0_attn_b_key_scale"], dtype=np.float32)

    idn = np.eye(128, dtype=np.float32).astype(BF)
    h128 = _hadamard(128).astype(BF)
    h64 = _hadamard(64).astype(BF)
    onesb = np.ones((128, 128), np.float32).astype(BF)
    ones1x128 = np.ones((1, 128), np.float32)
    iotaqd = np.broadcast_to(np.arange(1, BLK + 1, dtype=np.float32), (128, 256)).copy()
    p = np.arange(128, dtype=np.float32)
    iotakd = np.stack([255.0 - p, 127.0 - p], axis=1).copy()
    m = np.arange(256, dtype=np.float32)
    iotadT = np.ascontiguousarray(np.stack(
        [m[None, :] - p[:, None], m[None, :] - (p + 128.0)[:, None]], axis=1))
    tri = (iotadT >= 0).astype(np.float32)

    maps = []
    for core in range(NC):
        b, c = divmod(core, 4)
        seqx = x[b]

        def chunk(i):
            if 0 <= i < 4:
                return seqx[i * CH:(i + 1) * CH]
            return np.zeros((CH, DM), np.float32)

        xw_fm = np.ascontiguousarray(
            np.concatenate([chunk(c - 1), chunk(c), chunk(c + 1)], axis=0).T)
        xp_fm = np.ascontiguousarray(
            np.concatenate([chunk(c - 3), chunk(c - 2)], axis=0).T)

        w0 = (c - 1) * CH
        rc = np.zeros((WIN, DH), np.float32)
        rs = np.zeros((WIN, DH), np.float32)
        lo, hi = max(0, -w0), min(WIN, SEQ - w0)
        if hi > lo:
            rc[lo:hi] = cos_t[w0 + lo: w0 + hi]
            rs[lo:hi] = sinF[w0 + lo: w0 + hi]

        valid_chunks = [0 <= (c - 1 + j) < 4 for j in range(3)]
        bm = np.zeros((2, N_APPLY, 128, 512), np.float32)
        for qb in range(2):
            for i, kb in enumerate(_APPLY_KB[qb]):
                v = _BASEMASK[qb][kb].copy()
                if not valid_chunks[kb // 8]:
                    v[:] = 0.0
                bm[qb, i] = v

        maps.append({
            "xw_fm": xw_fm, "xp_fm": xp_fm,
            "wqkvP": wqkvP, "wqkvV": wqkvV, "wgP": wgP, "woP": woP,
            "w1P": w1P, "w3P": w3P, "w2T": w2T,
            "wbq": wbq, "wbk": wbk, "wbv": wbv, "wboP": wboP,
            "slope_r": slope.reshape(1, 16).copy(),
            "normw_pc": np.ascontiguousarray(normw.reshape(16, 128).T),
            "normw_row": normw.reshape(1, 2048).copy(),
            "qscale_rep": np.broadcast_to(qsc, (128, 128)).copy(),
            "kscale_rep": np.broadcast_to(ksc, (128, 128)).copy(),
            "ropecos": rc, "ropesinF": rs,
            "bmask": bm.astype(BF),
            "idn": idn, "h128": h128, "h64": h64, "onesb": onesb,
            "ones1x128": ones1x128,
            "iotaqd": iotaqd, "iotakd": iotakd,
            "iotadT": iotadT, "tri": tri,
        })
    return maps


ENGS = ("sync", "gpsimd", "tensor", "vector", "scalar")
T, V_, S, G, Y = "tensor", "vector", "scalar", "gpsimd", "sync"


class Emit:
    def __init__(self, nc, who, eng, sems):
        self.nc = nc
        self.who = who
        self.eng = eng
        self.sems = sems
        self.cnt = {k: 0 for k in sems}
        self.bank = {}

    def bump(self, sem, n):
        self.cnt[sem] += n
        return self.cnt[sem]

    def dma(self, q, out, in_, sem):
        self.bump(sem, 16)
        if self.who == q:
            self.eng.dma_start(out=out, in_=in_).then_inc(self.sems[sem], 16)

    def wait(self, who, sem, val=None):
        v = self.cnt[sem] if val is None else val
        if v <= 0:
            return
        if self.who == who:
            self.eng.wait_ge(self.sems[sem], v)

    def op(self, who, fn, sem=None, n=1):
        if sem is not None:
            self.bump(sem, n)
        if self.who == who:
            inst = fn(self.eng)
            if sem is not None:
                inst.then_inc(self.sems[sem], n)

    def drain(self, who):
        if self.who == who:
            self.eng.drain()

    def acq(self, bankname):
        st = self.bank.get(bankname)
        if st is not None:
            self.wait(T, st[0], st[1])

    def rel(self, bankname, sem):
        self.bank[bankname] = (sem, self.cnt[sem])

    def barrier(self):
        for s in ("s_pe", "s_v", "s_a", "s_sp", "s_out"):
            for w in ENGS:
                self.wait(w, s)


def build_nc(stop=STOP):
    nc = bass.Bass("TRN2", target_bir_lowering=False)
    D = {}

    def din(name, shape, dt=F32):
        D[name] = nc.dram_tensor(name, shape, dt, kind="ExternalInput").ap()

    def dint(name, shape, dt):
        D[name] = nc.dram_tensor(name, shape, dt, kind="Internal").ap()

    din("xw_fm", [DM, WIN]); din("xp_fm", [DM, PRE])
    din("wqkvP", [128, 32 * 2048]); din("wqkvV", [128, 16 * 2048])
    din("wgP", [128, 16 * 2048]); din("woP", [128, 16 * 2048])
    din("w1P", [128, 64 * 2048]); din("w3P", [128, 64 * 2048]); din("w2T", [HF, DM])
    din("wbq", [DM, DM]); din("wbk", [DM, 512]); din("wbv", [DM, 512]); din("wboP", [128, 16 * 2048])
    din("slope_r", [1, 16]); din("normw_pc", [128, 16]); din("normw_row", [1, DM])
    din("qscale_rep", [128, 128]); din("kscale_rep", [128, 128])
    din("ropecos", [WIN, DH]); din("ropesinF", [WIN, DH])
    din("bmask", [2, N_APPLY, 128, 512], BF16)
    din("idn", [128, 128], BF16); din("h128", [128, 128], BF16); din("h64", [64, 64], BF16)
    din("onesb", [128, 128], BF16); din("ones1x128", [1, 128])
    din("iotaqd", [128, 256]); din("iotakd", [128, 2])
    din("iotadT", [128, 2, 256]); din("tri", [128, 2, 256])

    dint("wqkvP_b", [128, 32 * 2048], BF16); dint("wqkvV_b", [128, 16 * 2048], BF16)
    dint("wgP_b", [128, 16 * 2048], BF16); dint("woP_b", [128, 16 * 2048], BF16)
    dint("w1P_b", [128, 64 * 2048], BF16); dint("w3P_b", [128, 64 * 2048], BF16)
    dint("at_b", [HF, DM], BF16); dint("w2ht_b", [128, 4 * 64 * 512], BF16)
    dint("x1d", [DM, WIN], F32); dint("x2d", [DM, WIN], F32); dint("x3d", [DM, CH], F32)
    dint("xn2d", [DM, WIN], BF16)
    dint("qtmd", [CH, DM], BF16)
    outy = nc.dram_tensor("outy", [DM, CH], F32, kind="ExternalOutput").ap()

    sem_names = ["s_in", "s_st", "s_ws", "s_sp", "s_pe", "s_v", "s_a", "s_out",
                 "s_x", "s_fx", "s_rp", "s_wv", "s_qt",
                 "s_bw0", "s_bw1", "sp0", "sp1", "sp2",
                 "pA0", "pA1", "pA2", "pB0", "pB1", "pC0", "pC1", "pC2",
                 "pG0", "pG1", "pG2"]

    with contextlib.ExitStack() as stk:
        def sb(name, shape, dt):
            return stk.enter_context(nc.sbuf_tensor(name, shape, dt))

        cidn = sb("cidn", [128, 128], BF16)
        ch128 = sb("ch128", [128, 128], BF16)
        ch64 = sb("ch64", [64, 64], BF16)
        conesb = sb("conesb", [128, 128], BF16)
        cones1 = sb("cones1", [1, 128], F32)
        cnormrow = sb("cnormrow", [1, DM], BF16)
        rstdb = sb("rstdb", [1, 512], BF16)
        cqsc = sb("cqsc", [128, 128], F32)
        cksc = sb("cksc", [128, 128], F32)
        ciotakd = sb("ciotakd", [128, 2], F32)
        cslope = sb("cslope", [1, 16], F32)
        nsrep = sb("nsrep", [128, 16], F32)
        bdrep = sb("bdrep", [128, 16], F32)
        kdect = sb("kdect", [128, NH, 2], F32)
        xf32 = sb("xf32", [128, 8192], F32)
        lnbf = sb("lnbf", [128, 2, 512], BF16)
        lnsq = sb("lnsq", [128, 2, 512], BF16)
        strow = sb("strow", [1, 2048], F32)
        repst = sb("repst", [128, 2, 512], F32)
        ftmp = sb("ftmp", [128, 2, 512], F32)
        fx2 = sb("fx2", [128, 1, 512], F32)
        wbufA = sb("wbufA", [128, 3, 2048], BF16)
        wbufB = sb("wbufB", [128, 2, 2048], BF16)
        silb = sb("silb", [128, 3, 512], BF16)
        ropeb = sb("ropeb", [128, 2, 2, 128], BF16)
        inv1 = sb("inv1", [1, 512], F32)
        conescol = sb("conescol", [128, 1], F32)
        arena = sb("arena", [128, 61440], BF16)
        invr = arena[:, 60416:61440].bitcast(F32)

        psn = [stk.enter_context(nc.psum_tensor(f"ps{i}", [128, 512], F32))
               for i in range(8)]
        sems = {s: stk.enter_context(nc.semaphore(s)) for s in sem_names}
        block = stk.enter_context(nc.Block())

        def av(off, n):
            return arena[:, off:off + n]

        XN = [av(0 + k * 512, 512) for k in range(16)]
        QF = [av(8192 + k * 512, 512) for k in range(16)]
        KF = [av(16384 + k * 512, 512) for k in range(16)]
        VT = [av(24576 + t * 2048, 2048) for t in range(4)]
        AO = [av(32768 + k * 512, 512) for k in range(16)]
        QKD = [[av(40960 + (p_ * 2 + s_) * 256, 256) for s_ in range(2)] for p_ in range(2)]
        QD2 = [av(41984 + p_ * 256, 256) for p_ in range(2)]
        KD2 = [[av(42496 + (p_ * 2 + s_) * 128, 128) for s_ in range(2)] for p_ in range(2)]
        GT = QF
        PR = KF
        XN2O = QF
        HID = [av(0 + h * 512, 512) for h in range(64)]
        XNF = [av(32768 + k * 512, 512) for k in range(16)]
        KFM = [av(0 + h * 3072, 3072) for h in range(4)]
        VTM = [av(12288 + t * 512, 512) for t in range(24)]
        XN3 = [av(24576 + k * 512, 512) for k in range(16)]
        KTM = av(32768, 512)
        KNR = av(33280, 512)
        QTM4 = [av(33792 + t * 2048, 2048) for t in range(4)]
        RSC1 = av(41984, 2048)
        RSC2 = av(44032, 2048)
        WVB = av(46080, 8192)
        QFMq = [av(33792 + h * 512, 512) for h in range(16)]
        EXPB = [av(41984 + i * 512, 512) for i in range(4)]
        AN = [av(44032 + k * 512, 512) for k in range(16)]
        BM = [av(52224 + i * 512, 512) for i in range(N_APPLY)]
        QTMR = av(58368, 2048)
        dmaskt = av(43008, 8192).rearrange("p (h s m) -> p h s m", s=2, m=256)
        qdect = av(51200, 4096).rearrange("p (h m) -> p h m", m=256)
        kvf = arena[:, 55296:55296 + 4096].bitcast(F32).rearrange("p (h e) -> p h e", e=128)
        kvb = av(59392, 2048).rearrange("p (h e) -> p h e", e=128)
        ciotaqd = arena[:, 55296:55296 + 512].bitcast(F32)
        ciotadT = arena[:, 55808:55808 + 1024].bitcast(F32).rearrange("p (s m) -> p s m", m=256)
        ctri = arena[:, 56832:56832 + 1024].bitcast(F32).rearrange("p (s m) -> p s m", m=256)

        XH = [xf32[:, k * 512:(k + 1) * 512] for k in range(16)]

        first_counts = {}

        def program(E: Emit):
            ps = psn

            E.dma(Y, cidn[:], D["idn"][:, :], "s_in")
            E.dma(Y, ch128[:], D["h128"][:, :], "s_in")
            E.dma(Y, ch64[:], D["h64"][:, :], "s_in")
            E.dma(Y, conesb[:], D["onesb"][:, :], "s_in")
            E.dma(Y, cones1[:], D["ones1x128"][:, :], "s_in")
            E.dma(G, cnormrow[:], D["normw_row"][:, :], "s_st")
            E.dma(Y, cqsc[:], D["qscale_rep"][:, :], "s_in")
            E.dma(Y, cksc[:], D["kscale_rep"][:, :], "s_in")
            E.dma(Y, ciotaqd[:], D["iotaqd"][:, :], "s_in")
            E.dma(Y, ciotakd[:], D["iotakd"][:, :], "s_in")
            E.dma(Y, ciotadT[:], D["iotadT"][:, :, :], "s_in")
            E.dma(Y, ctri[:], D["tri"][:, :, :], "s_in")
            E.dma(Y, cslope[:], D["slope_r"][:, :], "s_in")
            E.wait(T, "s_in")
            E.op(T, lambda t: t.matmul(ps[0][:, 0:16], cones1[:], cslope[:],
                                       start=True, stop=True), "s_pe")
            E.wait(V_, "s_pe")
            E.op(V_, lambda v: v.tensor_scalar_mul(nsrep[:], ps[0][:, 0:16], -1.0), "s_v")
            E.rel("ps0", "s_v")
            E.wait(S, "s_v")
            E.op(S, lambda s: s.activation(bdrep[:], nsrep[:], AF.Exp, scale=256.0), "s_a")
            for h in range(NH):
                E.op(S, lambda s, h=h: s.activation(
                    qdect[:, h, :], ciotaqd[:], AF.Exp, scale=nsrep[:, h:h + 1]), "s_a")
                E.op(S, lambda s, h=h: s.activation(
                    kdect[:, h, :], ciotakd[:], AF.Exp, scale=nsrep[:, h:h + 1]), "s_a")
                for su in range(2):
                    E.drain(S)
                    E.op(S, lambda s, h=h, su=su: s.activation(
                        ftmp[:, su, 0:256], ciotadT[:, su, :], AF.Exp,
                        scale=nsrep[:, h:h + 1]), "s_a")
                    E.wait(V_, "s_a")
                    E.drain(V_)
                    E.op(V_, lambda v, h=h, su=su: v.tensor_mul(
                        dmaskt[:, h, su, :], ftmp[:, su, 0:256], ctri[:, su, :]), "s_v")
                    E.wait(S, "s_v")
            E.drain(V_)
            E.op(V_, lambda v: v.memset(kvf[:], 0.0), "s_v")
            E.drain(V_)
            E.op(V_, lambda v: v.memset(conescol[:], 1.0), "s_v")
            E.wait(S, "s_v")
            E.op(S, lambda s: s.copy(kvb[:], kvf[:]), "s_a")

            a_marks = []
            for dblk in range(4):
                for cg in range(64):
                    it = dblk * 64 + cg
                    rot = it % 3
                    qq = G
                    buf = wbufA[:, rot, 0:512]
                    if it >= 3:
                        E.wait(qq, "s_pe", a_marks[it - 3])
                    E.dma(qq, buf, D["w2T"][cg * 128:(cg + 1) * 128,
                                            dblk * 512:(dblk + 1) * 512], f"pG{rot}")
                    E.wait(T, f"pG{rot}")
                    E.acq(f"ps{rot}")
                    E.op(T, lambda t, rot=rot, buf=buf: t.matmul(
                        ps[rot][:], ch128[:], buf, start=True, stop=True), "s_pe")
                    a_marks.append(E.cnt["s_pe"])
                    E.wait(S, "s_pe")
                    E.wait(S, f"sp{rot}")
                    E.drain(S)
                    E.op(S, lambda s, rot=rot: s.activation(
                        silb[:, rot, :], ps[rot][:], AF.Copy), "s_a")
                    E.rel(f"ps{rot}", "s_a")
                    E.wait(Y, "s_a")
                    E.dma(Y, D["at_b"][cg * 128:(cg + 1) * 128,
                                       dblk * 512:(dblk + 1) * 512],
                          silb[:, rot, :], f"sp{rot}")
            passa_sp = [E.cnt["sp0"], E.cnt["sp1"], E.cnt["sp2"]]
            atr = D["at_b"].rearrange("(cg p) d -> cg p d", p=128)
            for w in (Y, G, S):
                for si in range(3):
                    E.wait(w, f"sp{si}", passa_sp[si])
            w2out = D["w2ht_b"].rearrange("p (db hh d) -> p db hh d", db=4, d=512)
            PB = [[av((par * 4 + db) * 512, 512) for db in range(4)] for par in range(2)]
            b_marks = []
            for rj in range(128):
                par = rj % 2
                qq = (G, Y)[par]
                buf = wbufB[:, par, :][0:64, :]
                if rj >= 2:
                    E.wait(qq, "s_pe", b_marks[4 * (rj - 2) + 3])
                E.dma(qq, buf, atr[:, rj, :], f"pA{par}")
                E.wait(T, f"pA{par}")
                for db in range(4):
                    E.acq(f"ps{par * 4 + db}")
                    E.op(T, lambda t, par=par, db=db, buf=buf: t.matmul(
                        ps[par * 4 + db][0:64, :], ch64[:],
                        buf[:, db * 512:(db + 1) * 512], start=True, stop=True), "s_pe")
                    b_marks.append(E.cnt["s_pe"])
                E.wait(S, "s_pe")
                E.wait(S, f"s_bw{par}")
                for db in range(4):
                    E.drain(S)
                    E.op(S, lambda s, par=par, db=db: s.activation(
                        PB[par][db][0:64, :], ps[par * 4 + db][0:64, :], AF.Copy,
                        scale=FW_SCALE), "s_a")
                    E.rel(f"ps{par * 4 + db}", "s_a")
                E.wait(S, "s_a")
                for db in range(4):
                    E.dma(S, w2out[rj, db, :, :], PB[par][db][0:64, :], f"s_bw{par}")
            stage_mark = {}
            for nm, bnm, ncols in (("wqkvP", "wqkvP_b", 32 * 2048),
                                   ("wqkvV", "wqkvV_b", 16 * 2048),
                                   ("wgP", "wgP_b", 16 * 2048),
                                   ("woP", "woP_b", 16 * 2048),
                                   ("w1P", "w1P_b", 64 * 2048),
                                   ("w3P", "w3P_b", 64 * 2048)):
                nsl = max(4, ncols // 16384)
                for sl in range(nsl):
                    c0, c1 = sl * (ncols // nsl), (sl + 1) * (ncols // nsl)
                    E.dma(G, D[bnm][:, c0:c1], D[nm][:, c0:c1], "s_st")
                stage_mark[nm] = E.cnt["s_st"]
            stage_done_w = E.cnt["s_st"]

            w2ht_bw0 = E.cnt["s_bw0"]
            w2ht_bw1 = E.cnt["s_bw1"]
            for w in ENGS:
                E.wait(w, "s_bw0", w2ht_bw0)
                E.wait(w, "s_bw1", w2ht_bw1)
            E.barrier()

            def ln_fm(x_tiles, xn_tiles, eps):
                E.wait(S, "s_v")
                E.wait(T, "s_v")
                pe_marks = []
                for k in range(16):
                    if k >= 2:
                        E.wait(S, "s_pe", pe_marks[k - 2])
                    E.drain(S)
                    E.op(S, lambda s, k=k: s.activation(
                        lnsq[:, k % 2, :], x_tiles[k], AF.Square), "s_a")
                    E.wait(T, "s_a")
                    if k == 0:
                        E.acq("ps6"); E.acq("ps7")
                    E.op(T, lambda t, k=k: t.matmul(
                        ps[6][0:1, :], conescol[:], x_tiles[k],
                        start=(k == 0), stop=(k == 15)), "s_pe")
                    E.op(T, lambda t, k=k: t.matmul(
                        ps[7][0:1, :], conesb[:, 0:1], lnsq[:, k % 2, :],
                        start=(k == 0), stop=(k == 15)), "s_pe")
                    pe_marks.append(E.cnt["s_pe"])
                E.wait(V_, "s_pe")
                E.drain(V_)
                E.op(V_, lambda v: v.tensor_copy(strow[:, 0:512], ps[6][0:1, :]), "s_v")
                E.op(V_, lambda v: v.tensor_copy(strow[:, 512:1024], ps[7][0:1, :]), "s_v")
                E.rel("ps6", "s_v"); E.rel("ps7", "s_v")
                E.drain(V_)
                E.op(V_, lambda v: v.tensor_scalar_mul(strow[:, 0:512], strow[:, 0:512], 1.0 / DM), "s_v")
                E.op(V_, lambda v: v.tensor_scalar_mul(strow[:, 512:1024], strow[:, 512:1024], 1.0 / DM), "s_v")
                E.drain(V_)
                E.op(V_, lambda v: v.tensor_mul(strow[:, 1024:1536], strow[:, 0:512], strow[:, 0:512]), "s_v")
                E.drain(V_)
                E.op(V_, lambda v: v.tensor_sub(strow[:, 1024:1536], strow[:, 512:1024], strow[:, 1024:1536]), "s_v")
                E.drain(V_)
                E.op(V_, lambda v: v.tensor_scalar_add(strow[:, 1024:1536], strow[:, 1024:1536], float(eps)), "s_v")
                E.wait(S, "s_v")
                E.drain(S)
                E.op(S, lambda s: s.activation(strow[:, 1536:2048], strow[:, 1024:1536], AF.Sqrt), "s_a")
                E.wait(V_, "s_a")
                E.drain(V_)
                E.op(V_, lambda v: v.reciprocal(strow[:, 1024:1536], strow[:, 1536:2048]), "s_v")
                E.drain(V_)
                E.op(V_, lambda v: v.tensor_mul(strow[:, 512:1024], strow[:, 0:512], strow[:, 1024:1536]), "s_v")
                E.wait(T, "s_v")
                E.acq("ps6"); E.acq("ps7")
                E.op(T, lambda t: t.matmul(ps[6][:], cones1[:], strow[:, 1024:1536],
                                           start=True, stop=True), "s_pe")
                E.op(T, lambda t: t.matmul(ps[7][:], cones1[:], strow[:, 512:1024],
                                           start=True, stop=True), "s_pe")
                E.wait(V_, "s_pe")
                E.drain(V_)
                E.op(V_, lambda v: v.tensor_copy(repst[:, 0, :], ps[6][:]), "s_v")
                E.op(V_, lambda v: v.tensor_copy(repst[:, 1, :], ps[7][:]), "s_v")
                E.rel("ps6", "s_v"); E.rel("ps7", "s_v")
                for k in range(16):
                    E.drain(V_)
                    E.op(V_, lambda v, k=k: v.tensor_mul(ftmp[:, k % 2, :], x_tiles[k], repst[:, 0, :]), "s_v")
                    E.drain(V_)
                    E.op(V_, lambda v, k=k: v.tensor_sub(xn_tiles[k], ftmp[:, k % 2, :], repst[:, 1, :]), "s_v")

            def proj_c1(wdram, blk0, rhs_tiles, evac, st_mark=0):
                marks = []
                E.wait(Y, "s_pe")
                E.wait(Y, "s_st", st_mark)
                for oc in range(16):
                    rot = oc % 3
                    buf = wbufA[:, rot, :]
                    if oc >= 3:
                        E.wait(Y, "s_pe", marks[oc - 3])
                    j = blk0 + oc
                    E.dma(Y, buf, wdram[:, j * 2048:(j + 1) * 2048], f"pA{rot}")
                    E.wait(T, f"pA{rot}")
                    pb = f"ps{4 + rot}"
                    E.acq(pb)
                    for i in range(16):
                        E.op(T, lambda t, i=i, rot=rot, buf=buf: t.matmul(
                            ps[4 + rot][:],
                            buf.rearrange("p (i o) -> p i o", o=128)[:, i, :],
                            rhs_tiles[i], start=(i == 0), stop=(i == 15)),
                            "s_pe" if i == 15 else None)
                    marks.append(E.cnt["s_pe"])
                    evac(oc, 4 + rot)

            for si in range(5):
                prefix = si < 2
                for hf in range(2):
                    if si < 2:
                        base, coff = D["xp_fm"], si * CH + hf * 512
                    else:
                        base, coff = D["xw_fm"], (si - 2) * CH + hf * 512
                    wcol = (si - 2) * CH + hf * 512

                    E.barrier()
                    E.dma(Y, xf32[:].rearrange("p (k t) -> p k t", t=512),
                          base.rearrange("(k p) t -> p k t", p=128)[:, :, coff:coff + 512],
                          "s_x")
                    E.wait(V_, "s_x"); E.wait(S, "s_x"); E.wait(T, "s_x")
                    ln_fm(XH, XN, EPS_LN)
                    E.wait(T, "s_v")

                    def ev_k(oc, pi):
                        E.wait(S, "s_pe")
                        E.drain(S)
                        E.op(S, lambda s, oc=oc, pi=pi: s.activation(
                            KF[oc], ps[pi][:], AF.Silu), "s_a")
                        E.rel(f"ps{pi}", "s_a")
                    proj_c1(D["wqkvP_b"], 16, XN, ev_k, stage_mark["wqkvP"])

                    vpw = arena[:, 32768:40960].rearrange("p (i o) -> p i o", o=512)
                    E.wait(S, "s_st", stage_mark["wqkvV"])
                    for vc in range(4):
                        E.wait(S, "s_pe")
                        E.dma(S, vpw, D["wqkvV_b"][:, vc * 8192:(vc + 1) * 8192], "s_ws")
                        E.wait(T, "s_ws")
                        for tb in range(4):
                            pb = f"ps{4 + (tb % 2)}"
                            E.acq(pb)
                            for i in range(16):
                                E.op(T, lambda t, i=i, tb=tb: t.matmul(
                                    ps[4 + (tb % 2)][:],
                                    XN[i][:, tb * 128:(tb + 1) * 128],
                                    vpw[:, i, :], start=(i == 0), stop=(i == 15)),
                                    "s_pe" if i == 15 else None)
                            E.wait(S, "s_pe")
                            E.drain(S)
                            E.op(S, lambda s, tb=tb, vc=vc: s.activation(
                                VT[tb][:, vc * 512:(vc + 1) * 512],
                                ps[4 + (tb % 2)][:], AF.Silu), "s_a")
                            E.rel(f"ps{4 + (tb % 2)}", "s_a")

                    if not prefix:
                        def ev_q(oc, pi):
                            E.wait(S, "s_pe")
                            E.drain(S)
                            E.op(S, lambda s, oc=oc, pi=pi: s.activation(
                                QF[oc], ps[pi][:], AF.Silu), "s_a")
                            E.rel(f"ps{pi}", "s_a")
                        proj_c1(D["wqkvP_b"], 0, XN, ev_q, stage_mark["wqkvP"])

                    last_qkd_pe = [0, 0]
                    last_kd_pe = [0, 0]
                    for blk in range(2):
                        tb0 = blk * 2
                        E.wait(T, "s_a")
                        for h in range(NH):
                            par = h % 2
                            if not prefix:
                                for su in range(2):
                                    E.acq(f"ps{su}")
                                    E.op(T, lambda t, h=h, su=su, blk=blk: t.matmul(
                                        ps[su][:, 0:256],
                                        KF[h][:, blk * 256 + su * 128: blk * 256 + (su + 1) * 128],
                                        QF[h][:, blk * 256:(blk + 1) * 256],
                                        start=True, stop=True), "s_pe")
                                qk_pe = E.cnt["s_pe"]
                                E.wait(V_, "s_pe", max(qk_pe, last_qkd_pe[par]))
                                E.drain(V_)
                                for su in range(2):
                                    E.op(V_, lambda v, h=h, su=su, par=par: v.tensor_mul(
                                        QKD[par][su], ps[su][:, 0:256], dmaskt[:, h, su, :]), "s_v")
                                    E.rel(f"ps{su}", "s_v")
                                E.op(V_, lambda v, h=h, blk=blk, par=par: v.tensor_mul(
                                    QD2[par], QF[h][:, blk * 256:(blk + 1) * 256],
                                    qdect[:, h, :]), "s_v")
                                qkd_v = E.cnt["s_v"]
                            for su in range(2):
                                E.acq(f"ps{2 + su}")
                                E.op(T, lambda t, h=h, su=su, blk=blk: t.matmul(
                                    ps[2 + su][:, 0:128],
                                    KF[h][:, blk * 256 + su * 128: blk * 256 + (su + 1) * 128],
                                    cidn[:], start=True, stop=True), "s_pe")
                            kd_pe = E.cnt["s_pe"]
                            E.wait(S, "s_pe", max(kd_pe, last_kd_pe[par]))
                            for su in range(2):
                                E.drain(S)
                                E.op(S, lambda s, h=h, su=su, par=par: s.activation(
                                    KD2[par][su], ps[2 + su][:, 0:128], AF.Copy,
                                    scale=kdect[:, h, su:su + 1]), "s_a")
                                E.rel(f"ps{2 + su}", "s_a")
                            kd_a = E.cnt["s_a"]

                            if not prefix:
                                E.wait(T, "s_v", qkd_v)
                                E.acq("ps6")
                                for su in range(2):
                                    E.op(T, lambda t, h=h, su=su, tb0=tb0, par=par: t.matmul(
                                        ps[6][:, 0:256],
                                        VT[tb0 + su][:, h * 128:(h + 1) * 128],
                                        QKD[par][su], start=(su == 0), stop=False), None)
                                E.op(T, lambda t, h=h, par=par: t.matmul(
                                    ps[6][:, 0:256], kvb[:, h, :], QD2[par],
                                    start=False, stop=True), "s_pe")
                                last_qkd_pe[par] = E.cnt["s_pe"]
                                E.wait(S, "s_pe")
                                E.drain(S)
                                E.op(S, lambda s, h=h, blk=blk: s.activation(
                                    AO[h][:, blk * 256:(blk + 1) * 256],
                                    ps[6][:, 0:256], AF.Copy), "s_a")
                                E.rel("ps6", "s_a")

                            E.wait(T, "s_a", kd_a)
                            E.acq("ps7")
                            for su in range(2):
                                E.op(T, lambda t, h=h, su=su, tb0=tb0, par=par: t.matmul(
                                    ps[7][:, 0:128], KD2[par][su],
                                    VT[tb0 + su][:, h * 128:(h + 1) * 128],
                                    start=(su == 0), stop=(su == 1)),
                                    "s_pe" if su == 1 else None)
                            last_kd_pe[par] = E.cnt["s_pe"]
                            E.wait(V_, "s_pe")
                            E.drain(V_)
                            E.op(V_, lambda v, h=h: v.scalar_tensor_tensor(
                                kvf[:, h, :], kvf[:, h, :], bdrep[:, h:h + 1],
                                ps[7][:, 0:128], OP.mult, OP.add), "s_v")
                            E.rel("ps7", "s_v")
                            E.wait(S, "s_v")
                            E.drain(S)
                            E.op(S, lambda s, h=h: s.copy(kvb[:, h, :], kvf[:, h, :]), "s_a")

                    if prefix:
                        continue

                    def ev_g(oc, pi):
                        E.wait(S, "s_pe")
                        E.drain(S)
                        E.op(S, lambda s, oc=oc, pi=pi: s.activation(
                            GT[oc], ps[pi][:], AF.Sigmoid), "s_a")
                        E.rel(f"ps{pi}", "s_a")
                    proj_c1(D["wgP_b"], 0, XN, ev_g, stage_mark["wgP"])

                    pe_marks = []
                    for k in range(16):
                        if k >= 2:
                            E.wait(S, "s_pe", pe_marks[k - 2])
                        E.drain(S)
                        E.op(S, lambda s, k=k: s.activation(
                            lnsq[:, k % 2, :], AO[k], AF.Square), "s_a")
                        E.wait(T, "s_a")
                        if k == 0:
                            E.acq("ps6")
                        E.op(T, lambda t, k=k: t.matmul(
                            ps[6][0:1, :], conesb[:, 0:1], lnsq[:, k % 2, :],
                            start=(k == 0), stop=(k == 15)), "s_pe")
                        pe_marks.append(E.cnt["s_pe"])
                    E.wait(V_, "s_pe")
                    E.drain(V_)
                    E.op(V_, lambda v: v.tensor_scalar_mul(strow[:, 0:512], ps[6][0:1, :], 1.0 / DM), "s_v")
                    E.rel("ps6", "s_v")
                    E.drain(V_)
                    E.op(V_, lambda v: v.tensor_scalar_add(strow[:, 0:512], strow[:, 0:512], EPS_A), "s_v")
                    E.wait(S, "s_v")
                    E.drain(S)
                    E.op(S, lambda s: s.activation(strow[:, 512:1024], strow[:, 0:512], AF.Sqrt), "s_a")
                    E.wait(V_, "s_a")
                    E.drain(V_)
                    E.op(V_, lambda v: v.reciprocal(strow[:, 1024:1536], strow[:, 512:1024]), "s_v")
                    E.drain(V_)
                    E.op(V_, lambda v: v.tensor_copy(rstdb[:], strow[:, 1024:1536]), "s_v")
                    rms_v = E.cnt["s_v"]
                    E.wait(T, "s_v", rms_v)
                    E.wait(T, "s_st", 16)
                    for k in range(16):
                        pb = f"ps{4 + (k % 2)}"
                        E.acq(pb)
                        E.op(T, lambda t, k=k: t.matmul(
                            ps[4 + (k % 2)][:], cnormrow[:, k * 128:(k + 1) * 128],
                            rstdb[:], start=True, stop=True), "s_pe")
                        E.wait(V_, "s_pe")
                        E.drain(V_)
                        E.op(V_, lambda v, k=k: v.tensor_mul(
                            silb[:, k % 2, :], AO[k], ps[4 + (k % 2)][:]), "s_v")
                        E.rel(pb, "s_v")
                        E.drain(V_)
                        E.op(V_, lambda v, k=k: v.tensor_mul(PR[k], silb[:, k % 2, :], GT[k]), "s_v")
                    E.wait(T, "s_v")

                    def ev_o(oc, pi):
                        E.wait(V_, "s_pe")
                        E.drain(V_)
                        E.op(V_, lambda v, oc=oc, pi=pi: v.tensor_add(
                            XH[oc], XH[oc], ps[pi][:]), "s_v")
                        E.rel(f"ps{pi}", "s_v")
                    proj_c1(D["woP_b"], 0, PR, ev_o, stage_mark["woP"])
                    x1_v = E.cnt["s_v"]

                    E.wait(Y, "s_v", x1_v)
                    E.dma(Y, D["x1d"].rearrange("(k p) t -> p k t", p=128)[:, :, wcol:wcol + 512],
                          xf32[:].rearrange("p (k t) -> p k t", t=512), "s_sp")
                    ln_fm(XH, XN2O, EPS_LN)
                    E.wait(Y, "s_v")
                    E.dma(Y, D["xn2d"].rearrange("(k p) t -> p k t", p=128)[:, :, wcol:wcol + 512],
                          arena[:, 8192:16384].rearrange("p (k t) -> p k t", t=512), "s_sp")

            if stop == "x1":
                E.barrier()
                E.wait(Y, "s_sp")
                E.dma(Y, outy[:, :], D["x1d"][:, CH:2 * CH], "s_out")
                E.wait(Y, "s_out")
                return

            w1r = D["w1P_b"]
            w3r = D["w3P_b"]
            w2hr = D["w2ht_b"].rearrange("p (db hh d) -> p db hh d", db=4, d=512)

            def ffn_sweep(xn_tiles, res_dram, res_col, out_dram, out_col, store_sem):
                pe_w = []
                E.wait(G, "s_st", stage_mark["w1P"])
                E.wait(Y, "s_st", stage_mark["w1P"])
                E.wait(S, "s_st", stage_mark["w3P"])
                for hc in range(64):
                    rot = hc % 3
                    rotb = hc % 2
                    qq = (G, Y)[hc % 2]
                    bufa = wbufA[:, rot, :]
                    bufb = wbufB[:, rotb, :]
                    if hc >= 3:
                        E.wait(qq, "s_pe", pe_w[2 * (hc - 3)])
                    if hc >= 2:
                        E.wait(S, "s_pe", pe_w[2 * (hc - 2) + 1])
                    E.dma(qq, bufa, w1r[:, hc * 2048:(hc + 1) * 2048], f"pA{rot}")
                    E.dma(S, bufb, w3r[:, hc * 2048:(hc + 1) * 2048], f"pC{rotb}")
                    E.wait(T, f"pA{rot}"); E.wait(T, f"pC{rotb}")
                    pa, pb = f"ps{rot}", f"ps{3 + rotb}"
                    E.acq(pa); E.acq(pb)
                    for i in range(16):
                        E.op(T, lambda t, i=i, rot=rot, bufa=bufa: t.matmul(
                            ps[rot][:], bufa.rearrange("p (i o) -> p i o", o=128)[:, i, :],
                            xn_tiles[i], start=(i == 0), stop=(i == 15)),
                            "s_pe" if i == 15 else None)
                    pe_w.append(E.cnt["s_pe"])
                    for i in range(16):
                        E.op(T, lambda t, i=i, rotb=rotb, bufb=bufb: t.matmul(
                            ps[3 + rotb][:], bufb.rearrange("p (i o) -> p i o", o=128)[:, i, :],
                            xn_tiles[i], start=(i == 0), stop=(i == 15)),
                            "s_pe" if i == 15 else None)
                    pe_w.append(E.cnt["s_pe"])
                    E.wait(S, "s_pe", pe_w[-2])
                    E.wait(S, "s_v")
                    E.drain(S)
                    E.op(S, lambda s, rot=rot: s.activation(
                        silb[:, rot, :], ps[rot][:], AF.Silu), "s_a")
                    E.rel(pa, "s_a")
                    E.wait(V_, "s_a")
                    E.wait(V_, "s_pe", pe_w[-1])
                    E.drain(V_)
                    E.op(V_, lambda v, hc=hc, rot=rot, rotb=rotb: v.tensor_mul(
                        HID[hc], silb[:, rot, :], ps[3 + rotb][:]), "s_v")
                    E.rel(pb, "s_v")
                E.wait(T, "s_v")
                E.wait(Y, "s_pe")
                E.wait(G, "s_pe")
                h4_marks = []
                for dblk in range(4):
                    for pi in range(4):
                        E.acq(f"ps{pi}")
                    for h4 in range(16):
                        it4 = dblk * 16 + h4
                        rot = it4 % 3
                        qq2 = (Y, G)[it4 % 2]
                        buf = wbufA[:, rot, :]
                        if it4 >= 3:
                            E.wait(qq2, "s_pe", h4_marks[it4 - 3])
                        E.dma(qq2, buf[:, 0:2048].rearrange("p (hh d) -> p hh d", d=512),
                              w2hr[:, dblk, h4 * 4:(h4 + 1) * 4, :], f"pA{rot}")
                        E.wait(T, f"pA{rot}")
                        for hh in range(4):
                            h = h4 * 4 + hh
                            for dsub in range(4):
                                E.op(T, lambda t, h=h, hh=hh, dsub=dsub, buf=buf: t.matmul(
                                    ps[dsub][:],
                                    buf.rearrange("p (hh d) -> p hh d", d=512)[:, hh, dsub * 128:(dsub + 1) * 128],
                                    HID[h], start=(h == 0), stop=(h == 63)),
                                    "s_pe" if dsub == 3 else None)
                        h4_marks.append(E.cnt["s_pe"])
                    w2_pe = E.cnt["s_pe"]
                    for dsub in range(4):
                        d = dblk * 4 + dsub
                        E.wait(Y, "s_v")
                        E.dma(Y, fx2[:, 0, :],
                              res_dram[d * 128:(d + 1) * 128, res_col:res_col + 512], "s_fx")
                        E.wait(V_, "s_pe", w2_pe)
                        E.wait(V_, "s_fx")
                        E.wait(V_, store_sem)
                        E.drain(V_)
                        E.op(V_, lambda v, dsub=dsub: v.tensor_add(
                            ftmp[:, dsub % 2, :], ps[dsub][:], fx2[:, 0, :]), "s_v")
                        E.rel(f"ps{dsub}", "s_v")
                        E.wait(Y, "s_v")
                        E.dma(Y, out_dram[d * 128:(d + 1) * 128, out_col:out_col + 512],
                              ftmp[:, dsub % 2, :], store_sem)

            E.wait(Y, "s_bw0", w2ht_bw0)
            E.wait(Y, "s_bw1", w2ht_bw1)
            for g in range(WIN // FG):
                col = g * FG
                E.barrier()
                E.dma(Y, arena[:, 32768:40960].rearrange("p (k t) -> p k t", t=512),
                      D["xn2d"].rearrange("(k p) t -> p k t", p=128)[:, :, col:col + 512], "s_x")
                E.wait(T, "s_x"); E.wait(V_, "s_x"); E.wait(S, "s_x")
                ffn_sweep(XNF, D["x1d"], col, D["x2d"], col, "s_sp")

            if stop == "x2":
                E.barrier()
                E.wait(Y, "s_sp")
                E.dma(Y, outy[:, :], D["x2d"][:, CH:2 * CH], "s_out")
                E.wait(Y, "s_out")
                return

            def rope_norm(dst, heads, wt0, scale_rep):
                rot = (wt0 // 128) % 2
                E.wait(G, "s_v")
                E.dma(G, ropeb[:, rot, 0, :], D["ropecos"][wt0:wt0 + 128, :], "s_rp")
                E.dma(G, ropeb[:, rot, 1, :], D["ropesinF"][wt0:wt0 + 128, :], "s_rp")
                E.wait(V_, "s_rp")
                W = heads * 128
                dv = dst.rearrange("p (h j) -> p h j", j=128)
                cosv = ropeb[:, rot, 0, :].unsqueeze(1).broadcast_to([128, heads, 128])
                sinL = ropeb[:, rot, 1, 0:64].unsqueeze(1).broadcast_to([128, heads, 64])
                sinH = ropeb[:, rot, 1, 64:128].unsqueeze(1).broadcast_to([128, heads, 64])
                t1 = RSC1[:, 0:W].rearrange("p (h j) -> p h j", j=128)
                t2 = RSC2[:, 0:W].rearrange("p (h j) -> p h j", j=128)
                E.drain(V_)
                E.op(V_, lambda v: v.tensor_mul(t1[:, :, 0:64], dv[:, :, 64:128], sinL), "s_v")
                E.op(V_, lambda v: v.tensor_mul(t1[:, :, 64:128], dv[:, :, 0:64], sinH), "s_v")
                E.drain(V_)
                E.op(V_, lambda v: v.tensor_mul(t2, dv, cosv), "s_v")
                E.drain(V_)
                E.op(V_, lambda v: v.tensor_add(dv, t1, t2), "s_v")
                E.wait(S, "s_v")
                E.drain(S)
                E.op(S, lambda s: s.activation(RSC1[:, 0:W], dst, AF.Square), "s_a")
                E.wait(V_, "s_a")
                E.drain(V_)
                sq = RSC1[:, 0:W].rearrange("p (h j) -> p h j", j=128)
                E.op(V_, lambda v: v.reduce_sum(
                    ftmp[:, 0, 0:heads].rearrange("p (h o) -> p h o", o=1), sq,
                    axis=mybir.AxisListType.X), "s_v")
                E.wait(S, "s_v")
                E.drain(S)
                E.op(S, lambda s: s.activation(ftmp[:, 0, 16:16 + heads], ftmp[:, 0, 0:heads], AF.Sqrt), "s_a")
                E.wait(V_, "s_a")
                E.drain(V_)
                E.op(V_, lambda v: v.tensor_scalar_add(ftmp[:, 0, 32:32 + heads], ftmp[:, 0, 16:16 + heads], EPS_A), "s_v")
                E.drain(V_)
                E.op(V_, lambda v: v.reciprocal(ftmp[:, 0, 48:48 + heads], ftmp[:, 0, 32:32 + heads]), "s_v")
                E.drain(V_)
                invv = ftmp[:, 0, 48:48 + heads].rearrange("p (h o) -> p h o", o=1).broadcast_to([128, heads, 128])
                E.op(V_, lambda v: v.tensor_mul(dv, dv, invv), "s_v")
                E.drain(V_)
                scv = scale_rep[:, :].unsqueeze(1).broadcast_to([128, heads, 128])
                E.op(V_, lambda v: v.tensor_mul(dv, dv, scv), "s_v")

            wbkr = D["wbk"].rearrange("(i p) o -> p i o", p=128)
            wbvr = D["wbv"].rearrange("(i p) o -> p i o", p=128)
            wbqr = D["wbq"].rearrange("(i p) o -> p i o", p=128)
            wvbv = WVB.rearrange("p (i o) -> p i o", o=512)

            for g in range(6):
                col = g * 512
                E.barrier()
                E.wait(Y, "s_sp")
                E.dma(Y, xf32[:].rearrange("p (k t) -> p k t", t=512),
                      D["x2d"].rearrange("(k p) t -> p k t", p=128)[:, :, col:col + 512], "s_x")
                E.wait(V_, "s_x"); E.wait(S, "s_x"); E.wait(T, "s_x")
                ln_fm(XH, XN3, EPS_LN)
                E.wait(T, "s_v")

                E.wait(G, "s_pe")
                E.dma(G, wvbv, wbkr[:, :, :], "s_wv")
                E.wait(T, "s_wv")
                for tb in range(4):
                    pb = f"ps{4 + (tb % 2)}"
                    E.acq(pb)
                    for i in range(16):
                        E.op(T, lambda t, i=i, tb=tb: t.matmul(
                            ps[4 + (tb % 2)][:], XN3[i][:, tb * 128:(tb + 1) * 128],
                            wvbv[:, i, :], start=(i == 0), stop=(i == 15)),
                            "s_pe" if i == 15 else None)
                    E.wait(S, "s_pe")
                    E.drain(S)
                    E.op(S, lambda s: s.activation(KTM, ps[4 + (tb % 2)][:], AF.Copy), "s_a")
                    E.rel(pb, "s_a")
                    E.wait(V_, "s_a")
                    rope_norm(KTM, 4, col + tb * 128, cksc)
                    kn_v = E.cnt["s_v"]
                    E.wait(T, "s_v", kn_v)
                    for kvh in range(4):
                        pq = f"ps{kvh % 2}"
                        E.acq(pq)
                        E.op(T, lambda t, kvh=kvh: t.matmul(
                            ps[kvh % 2][:, 0:128], KTM[:, kvh * 128:(kvh + 1) * 128],
                            cidn[:], start=True, stop=True), "s_pe")
                        E.wait(S, "s_pe")
                        E.drain(S)
                        E.op(S, lambda s, kvh=kvh, g=g, tb=tb: s.activation(
                            KFM[kvh][:, col + tb * 128: col + tb * 128 + 128],
                            ps[kvh % 2][:, 0:128], AF.Copy), "s_a")
                        E.rel(pq, "s_a")

                E.wait(G, "s_a")
                E.dma(G, wvbv, wbvr[:, :, :], "s_wv")
                E.wait(T, "s_wv")
                for tb in range(4):
                    pb = f"ps{4 + (tb % 2)}"
                    E.acq(pb)
                    for i in range(16):
                        E.op(T, lambda t, i=i, tb=tb: t.matmul(
                            ps[4 + (tb % 2)][:], XN3[i][:, tb * 128:(tb + 1) * 128],
                            wvbv[:, i, :], start=(i == 0), stop=(i == 15)),
                            "s_pe" if i == 15 else None)
                    E.wait(S, "s_pe")
                    E.drain(S)
                    E.op(S, lambda s, g=g, tb=tb: s.activation(
                        VTM[g * 4 + tb], ps[4 + (tb % 2)][:], AF.Copy), "s_a")
                    E.rel(pb, "s_a")

                if g in (2, 3):
                    for cb in range(4):
                        E.wait(G, "s_a")
                        E.dma(G, wvbv, wbqr[:, :, cb * 512:(cb + 1) * 512], "s_wv")
                        E.wait(T, "s_wv")
                        for tb in range(4):
                            pb = f"ps{4 + (tb % 2)}"
                            E.acq(pb)
                            for i in range(16):
                                E.op(T, lambda t, i=i, tb=tb: t.matmul(
                                    ps[4 + (tb % 2)][:], XN3[i][:, tb * 128:(tb + 1) * 128],
                                    wvbv[:, i, :], start=(i == 0), stop=(i == 15)),
                                    "s_pe" if i == 15 else None)
                            E.wait(S, "s_pe")
                            E.drain(S)
                            E.op(S, lambda s, tb=tb, cb=cb: s.activation(
                                QTM4[tb][:, cb * 512:(cb + 1) * 512],
                                ps[4 + (tb % 2)][:], AF.Copy), "s_a")
                            E.rel(pb, "s_a")
                    for tb in range(4):
                        E.wait(V_, "s_a")
                        rope_norm(QTM4[tb], 16, col + tb * 128, cqsc)
                        E.wait(Y, "s_v")
                        E.dma(Y, D["qtmd"][(g - 2) * 512 + tb * 128:(g - 2) * 512 + (tb + 1) * 128, :],
                              QTM4[tb], "s_sp")

            if stop == "x3_pre":
                E.barrier()
                E.wait(Y, "s_sp")
                E.dma(Y, outy[:, :], D["x2d"][:, CH:2 * CH], "s_out")
                E.wait(Y, "s_out")
                return

            wbor = D["wboP"]
            for qb in range(2):
                E.barrier()
                E.wait(Y, "s_sp")
                E.dma(Y, arena[:, 52224:52224 + N_APPLY * 512].rearrange(
                    "p (a t) -> p a t", t=512),
                    D["bmask"][qb].rearrange("a p t -> p a t"), "s_x")
                for sub in range(4):
                    E.wait(Y, "s_a")
                    E.dma(Y, QTMR, D["qtmd"][qb * 512 + sub * 128: qb * 512 + (sub + 1) * 128, :], "s_qt")
                    E.wait(T, "s_qt")
                    for h in range(NH):
                        pq = f"ps{h % 2}"
                        E.acq(pq)
                        E.op(T, lambda t, h=h: t.matmul(
                            ps[h % 2][:, 0:128], QTMR[:, h * 128:(h + 1) * 128],
                            cidn[:], start=True, stop=True), "s_pe")
                        E.wait(S, "s_pe")
                        E.drain(S)
                        E.op(S, lambda s, h=h, sub=sub: s.activation(
                            QFMq[h][:, sub * 128:(sub + 1) * 128],
                            ps[h % 2][:, 0:128], AF.Copy), "s_a")
                        E.rel(pq, "s_a")
                E.wait(T, "s_a"); E.wait(V_, "s_x")

                for h in range(NH):
                    kvh = h // 4
                    apply_pos = {kb: i for i, kb in enumerate(_APPLY_KB[qb])}
                    E.acq("ps2"); E.acq("ps3")
                    exp_marks = [0, 0, 0, 0]
                    tiles = list(enumerate(_KTILES[qb]))
                    score_marks = {}

                    def emit_score(idx, kb):
                        rot = idx % 2
                        E.acq(f"ps{rot}")
                        E.op(T, lambda t, kb=kb, kvh=kvh, h=h, rot=rot: t.matmul(
                            ps[rot][:], KFM[kvh][:, kb * 128:(kb + 1) * 128],
                            QFMq[h], start=True, stop=True), "s_pe")
                        score_marks[idx] = E.cnt["s_pe"]

                    emit_score(0, tiles[0][1])
                    for idx, kb in tiles:
                        if idx + 1 < len(tiles):
                            emit_score(idx + 1, tiles[idx + 1][1])
                        rot = idx % 2
                        er = idx % 4
                        pq = f"ps{rot}"
                        E.wait(S, "s_pe", max(score_marks[idx], exp_marks[er]))
                        E.drain(S)
                        E.op(S, lambda s, rot=rot: s.activation(
                            ftmp[:, rot, :], ps[rot][:], AF.Tanh, scale=CAPS), "s_a")
                        E.rel(pq, "s_a")
                        E.drain(S)
                        E.op(S, lambda s, rot=rot, er=er: s.activation(
                            EXPB[er], ftmp[:, rot, :], AF.Exp, scale=50.0), "s_a")
                        exp_a = E.cnt["s_a"]
                        masked = kb in apply_pos
                        if masked:
                            E.wait(V_, "s_a", exp_a)
                            E.drain(V_)
                            E.op(V_, lambda v, er=er, kb=kb: v.tensor_mul(
                                EXPB[er], EXPB[er], BM[apply_pos[kb]]), "s_v")
                            E.wait(T, "s_v")
                        else:
                            E.wait(T, "s_a", exp_a)
                        first, last = idx == 0, idx == N_KT - 1
                        E.op(T, lambda t, er=er, first=first, last=last: t.matmul(
                            ps[3][0:1, :], conesb[:, 0:1], EXPB[er],
                            start=first, stop=last), None)
                        E.op(T, lambda t, er=er, kb=kb, kvh=kvh, first=first, last=last: t.matmul(
                            ps[2][:], VTM[kb][:, kvh * 128:(kvh + 1) * 128], EXPB[er],
                            start=first, stop=last),
                            "s_pe")
                        exp_marks[er] = E.cnt["s_pe"]
                    av_pe = E.cnt["s_pe"]
                    E.wait(V_, "s_pe", av_pe)
                    E.drain(V_)
                    E.op(V_, lambda v: v.reciprocal(inv1[:], ps[3][0:1, :]), "s_v")
                    E.rel("ps3", "s_v")
                    E.wait(T, "s_v")
                    E.acq("ps4")
                    E.op(T, lambda t: t.matmul(ps[4][:], cones1[:], inv1[:],
                                               start=True, stop=True), "s_pe")
                    E.wait(V_, "s_pe")
                    E.drain(V_)
                    E.op(V_, lambda v: v.tensor_copy(invr[:], ps[4][:]), "s_v")
                    E.rel("ps4", "s_v")
                    E.drain(V_)
                    E.op(V_, lambda v, h=h: v.tensor_mul(AN[h], ps[2][:], invr[:]), "s_v")
                    E.rel("ps2", "s_v")
                E.wait(T, "s_v")

                omarks = []
                E.wait(G, "s_pe")
                for oc in range(16):
                    rot = oc % 2
                    buf = wbufA[:, rot, :]
                    if oc >= 2:
                        E.wait(G, "s_pe", omarks[oc - 2])
                    E.dma(G, buf, wbor[:, oc * 2048:(oc + 1) * 2048], f"pG{rot}")
                    E.wait(T, f"pG{rot}")
                    pb = f"ps{4 + rot}"
                    E.acq(pb)
                    for i in range(16):
                        E.op(T, lambda t, i=i, rot=rot, buf=buf: t.matmul(
                            ps[4 + rot][:],
                            buf.rearrange("p (i o) -> p i o", o=128)[:, i, :],
                            AN[i], start=(i == 0), stop=(i == 15)),
                            "s_pe" if i == 15 else None)
                    omarks.append(E.cnt["s_pe"])
                    E.wait(Y, "s_v")
                    E.dma(Y, fx2[:, 0, :],
                          D["x2d"][oc * 128:(oc + 1) * 128, CH + qb * 512: CH + qb * 512 + 512],
                          "s_fx")
                    E.wait(V_, "s_pe")
                    E.wait(V_, "s_fx")
                    E.wait(V_, "s_sp")
                    E.drain(V_)
                    E.op(V_, lambda v, rot=rot: v.tensor_add(
                        ftmp[:, rot, :], ps[4 + rot][:], fx2[:, 0, :]), "s_v")
                    E.rel(pb, "s_v")
                    E.wait(Y, "s_v")
                    E.dma(Y, D["x3d"][oc * 128:(oc + 1) * 128, qb * 512: qb * 512 + 512],
                          ftmp[:, rot, :], "s_sp")

            if stop == "x3":
                E.barrier()
                E.wait(Y, "s_sp")
                E.dma(Y, outy[:, :], D["x3d"][:, :], "s_out")
                E.wait(Y, "s_out")
                return

            for g in range(CH // FG):
                col = g * FG
                E.barrier()
                E.wait(Y, "s_sp")
                E.dma(Y, xf32[:].rearrange("p (k t) -> p k t", t=512),
                      D["x3d"].rearrange("(k p) t -> p k t", p=128)[:, :, col:col + 512], "s_x")
                E.wait(V_, "s_x"); E.wait(S, "s_x"); E.wait(T, "s_x")
                ln_fm(XH, XNF, EPS_LN)
                E.wait(T, "s_v")
                ffn_sweep(XNF, D["x3d"], col, outy, col, "s_out")

            E.barrier()
            E.wait(Y, "s_out")

        def run_pass(who, eng):
            em = Emit(nc, who, eng, sems)
            program(em)
            if not first_counts:
                first_counts.update(em.cnt)
            else:
                assert em.cnt == first_counts, (who, em.cnt, first_counts)

        @block.sync
        def _(e):
            run_pass("sync", e)

        @block.gpsimd
        def _(e):
            run_pass("gpsimd", e)

        @block.tensor
        def _(e):
            run_pass("tensor", e)

        @block.vector
        def _(e):
            run_pass("vector", e)

        @block.scalar
        def _(e):
            run_pass("scalar", e)

    return nc


def kernel(**inputs):
    maps = shard_inputs(inputs)
    nc = build_nc()
    res = run_bass_kernel_spmd(nc, maps, core_ids=list(range(NC)))
    out = np.zeros((BATCH, SEQ, DM), np.float32)
    for core in range(NC):
        b, c = divmod(core, 4)
        out[b, c * CH:(c + 1) * CH] = np.asarray(res.results[core]["outy"]).T
    return out


# version 07
# speedup vs baseline: 1.0540x; 1.0396x over previous
import contextlib
import numpy as np
import ml_dtypes
import concourse.bass as bass
import concourse.mybir as mybir
from concourse.bass_utils import run_bass_kernel_spmd

F32 = mybir.dt.float32
BF16 = mybir.dt.bfloat16
AF = mybir.ActivationFunctionType
OP = mybir.AluOpType

DM = 2048
NH = 16
DH = 128
BATCH = 2
SEQ = 4096
CH = 1024
WIN = 3 * CH
PRE = 2 * CH
BLK = 256
HF = 8192
NC = 8
EPS_LN = 1e-5
EPS_A = 1e-6
FG = 512
CAPS = float(1.0 / (50.0 * np.sqrt(128.0)))
FW_SCALE = float(1.0 / np.sqrt(8192.0))

STOP = "full"

BF = ml_dtypes.bfloat16


def _win_masks():
    ktiles, apply_kb, base = [], [], []
    for qb in range(2):
        ks, ap, bm = [], [], {}
        ql = np.arange(qb * 512, qb * 512 + 512)
        for kb in range(24):
            kl = np.arange(kb * 128, kb * 128 + 128)
            valid = (kl[:, None] >= ql[None, :]) & (kl[:, None] <= ql[None, :] + 2048)
            if not valid.any():
                continue
            ks.append(kb)
            if kb < 8 or kb >= 16:
                ap.append(kb)
                bm[kb] = valid.astype(np.float32)
        ktiles.append(ks)
        apply_kb.append(ap)
        base.append(bm)
    return ktiles, apply_kb, base


_KTILES, _APPLY_KB, _BASEMASK = _win_masks()
N_KT = len(_KTILES[0])
N_APPLY = len(_APPLY_KB[0])
assert len(_KTILES[1]) == N_KT and len(_APPLY_KB[1]) == N_APPLY


def _rope_tables():
    half = DH // 2
    inv_freq = 1.0 / 10000.0 ** (np.arange(0, half, dtype=np.float64) / half)
    ang = np.outer(np.arange(SEQ, dtype=np.float64), inv_freq)
    ang = np.concatenate([ang, ang], axis=-1)
    return np.cos(ang).astype(np.float32), np.sin(ang).astype(np.float32)


def _hadamard(n):
    h = np.array([[1.0]], dtype=np.float32)
    while h.shape[0] < n:
        h = np.block([[h, h], [h, -h]])
    return h


def shard_inputs(d):
    x = np.asarray(d["L0_attn_a_hidden_states"], dtype=np.float32)
    cos_t, sin_t = _rope_tables()
    sinF = sin_t.copy()
    sinF[:, :DH // 2] = -sin_t[:, :DH // 2]

    def pmaj(wT):
        rows, O = wT.shape
        i = rows // 128
        a = wT.reshape(i, 128, O // 128, 128)
        return np.ascontiguousarray(a.transpose(1, 2, 0, 3).reshape(128, (O // 128) * i * 128))

    def pmaj512(wT):
        rows, O = wT.shape
        i = rows // 128
        a = wT.reshape(i, 128, O // 512, 512)
        return np.ascontiguousarray(a.transpose(1, 2, 0, 3).reshape(128, (O // 512) * i * 512))

    wqkvT = np.ascontiguousarray(np.asarray(d["L0_attn_a_w_qkv"], dtype=np.float32).T)
    wqkvP = pmaj(wqkvT[:, :2 * DM])
    wqkvV = pmaj512(wqkvT[:, 2 * DM:])
    wgP = pmaj(np.asarray(d["L0_attn_a_w_gate"], dtype=np.float32).T)
    woP = pmaj(np.asarray(d["L0_attn_a_w_out"], dtype=np.float32).T)
    w1P = pmaj(np.asarray(d["L0_ffn_W1"], dtype=np.float32).T)
    w3P = pmaj(np.asarray(d["L0_ffn_W3"], dtype=np.float32).T)
    w2T = np.ascontiguousarray(np.asarray(d["L0_ffn_W2"], dtype=np.float32).T)
    wbq = np.ascontiguousarray(np.asarray(d["L0_attn_b_wq"], dtype=np.float32))
    wbk = np.ascontiguousarray(np.asarray(d["L0_attn_b_wk"], dtype=np.float32))
    wbv = np.ascontiguousarray(np.asarray(d["L0_attn_b_wv"], dtype=np.float32))
    wboP = pmaj(np.asarray(d["L0_attn_b_wo"], dtype=np.float32))
    slope = np.asarray(d["L0_attn_a_slope_rate"], dtype=np.float32)
    normw = np.asarray(d["L0_attn_a_norm_w"], dtype=np.float32)
    qsc = np.asarray(d["L0_attn_b_query_scale"], dtype=np.float32)
    ksc = np.asarray(d["L0_attn_b_key_scale"], dtype=np.float32)

    idn = np.eye(128, dtype=np.float32).astype(BF)
    h128 = _hadamard(128).astype(BF)
    h64 = _hadamard(64).astype(BF)
    onesb = np.ones((128, 128), np.float32).astype(BF)
    ones1x128 = np.ones((1, 128), np.float32)
    iotaqd = np.broadcast_to(np.arange(1, BLK + 1, dtype=np.float32), (128, 256)).copy()
    p = np.arange(128, dtype=np.float32)
    iotakd = np.stack([255.0 - p, 127.0 - p], axis=1).copy()
    m = np.arange(256, dtype=np.float32)
    iotadT = np.ascontiguousarray(np.stack(
        [m[None, :] - p[:, None], m[None, :] - (p + 128.0)[:, None]], axis=1))
    tri = (iotadT >= 0).astype(np.float32)

    maps = []
    for core in range(NC):
        b, c = divmod(core, 4)
        seqx = x[b]

        def chunk(i):
            if 0 <= i < 4:
                return seqx[i * CH:(i + 1) * CH]
            return np.zeros((CH, DM), np.float32)

        xw_fm = np.ascontiguousarray(
            np.concatenate([chunk(c - 1), chunk(c), chunk(c + 1)], axis=0).T)
        xp_fm = np.ascontiguousarray(
            np.concatenate([chunk(c - 3), chunk(c - 2)], axis=0).T)

        w0 = (c - 1) * CH
        rc = np.zeros((WIN, DH), np.float32)
        rs = np.zeros((WIN, DH), np.float32)
        lo, hi = max(0, -w0), min(WIN, SEQ - w0)
        if hi > lo:
            rc[lo:hi] = cos_t[w0 + lo: w0 + hi]
            rs[lo:hi] = sinF[w0 + lo: w0 + hi]

        valid_chunks = [0 <= (c - 1 + j) < 4 for j in range(3)]
        bm = np.zeros((2, N_APPLY, 128, 512), np.float32)
        for qb in range(2):
            for i, kb in enumerate(_APPLY_KB[qb]):
                v = _BASEMASK[qb][kb].copy()
                if not valid_chunks[kb // 8]:
                    v[:] = 0.0
                bm[qb, i] = v

        maps.append({
            "xw_fm": xw_fm, "xp_fm": xp_fm,
            "wqkvP": wqkvP, "wqkvV": wqkvV, "wgP": wgP, "woP": woP,
            "w1P": w1P, "w3P": w3P, "w2T": w2T,
            "wbq": wbq, "wbk": wbk, "wbv": wbv, "wboP": wboP,
            "slope_r": slope.reshape(1, 16).copy(),
            "normw_pc": np.ascontiguousarray(normw.reshape(16, 128).T),
            "normw_row": normw.reshape(1, 2048).copy(),
            "qscale_rep": np.broadcast_to(qsc, (128, 128)).copy(),
            "kscale_rep": np.broadcast_to(ksc, (128, 128)).copy(),
            "ropecos": rc, "ropesinF": rs,
            "bmask": bm.astype(BF),
            "idn": idn, "h128": h128, "h64": h64, "onesb": onesb,
            "ones1x128": ones1x128,
            "iotaqd": iotaqd, "iotakd": iotakd,
            "iotadT": iotadT, "tri": tri,
        })
    return maps


ENGS = ("sync", "gpsimd", "tensor", "vector", "scalar")
T, V_, S, G, Y = "tensor", "vector", "scalar", "gpsimd", "sync"


class Emit:
    def __init__(self, nc, who, eng, sems):
        self.nc = nc
        self.who = who
        self.eng = eng
        self.sems = sems
        self.cnt = {k: 0 for k in sems}
        self.bank = {}

    def bump(self, sem, n):
        self.cnt[sem] += n
        return self.cnt[sem]

    def dma(self, q, out, in_, sem):
        self.bump(sem, 16)
        if self.who == q:
            self.eng.dma_start(out=out, in_=in_).then_inc(self.sems[sem], 16)

    def wait(self, who, sem, val=None):
        v = self.cnt[sem] if val is None else val
        if v <= 0:
            return
        if self.who == who:
            self.eng.wait_ge(self.sems[sem], v)

    def op(self, who, fn, sem=None, n=1):
        if sem is not None:
            self.bump(sem, n)
        if self.who == who:
            inst = fn(self.eng)
            if sem is not None:
                inst.then_inc(self.sems[sem], n)

    def drain(self, who):
        if self.who == who:
            self.eng.drain()

    def acq(self, bankname):
        st = self.bank.get(bankname)
        if st is not None:
            self.wait(T, st[0], st[1])

    def rel(self, bankname, sem):
        self.bank[bankname] = (sem, self.cnt[sem])

    def barrier(self):
        for s in ("s_pe", "s_v", "s_a", "s_sp", "s_out"):
            for w in ENGS:
                self.wait(w, s)


def build_nc(stop=STOP):
    nc = bass.Bass("TRN2", target_bir_lowering=False)
    D = {}

    def din(name, shape, dt=F32):
        D[name] = nc.dram_tensor(name, shape, dt, kind="ExternalInput").ap()

    def dint(name, shape, dt):
        D[name] = nc.dram_tensor(name, shape, dt, kind="Internal").ap()

    din("xw_fm", [DM, WIN]); din("xp_fm", [DM, PRE])
    din("wqkvP", [128, 32 * 2048]); din("wqkvV", [128, 16 * 2048])
    din("wgP", [128, 16 * 2048]); din("woP", [128, 16 * 2048])
    din("w1P", [128, 64 * 2048]); din("w3P", [128, 64 * 2048]); din("w2T", [HF, DM])
    din("wbq", [DM, DM]); din("wbk", [DM, 512]); din("wbv", [DM, 512]); din("wboP", [128, 16 * 2048])
    din("slope_r", [1, 16]); din("normw_pc", [128, 16]); din("normw_row", [1, DM])
    din("qscale_rep", [128, 128]); din("kscale_rep", [128, 128])
    din("ropecos", [WIN, DH]); din("ropesinF", [WIN, DH])
    din("bmask", [2, N_APPLY, 128, 512], BF16)
    din("idn", [128, 128], BF16); din("h128", [128, 128], BF16); din("h64", [64, 64], BF16)
    din("onesb", [128, 128], BF16); din("ones1x128", [1, 128])
    din("iotaqd", [128, 256]); din("iotakd", [128, 2])
    din("iotadT", [128, 2, 256]); din("tri", [128, 2, 256])

    dint("wqkvP_b", [128, 32 * 2048], BF16); dint("wqkvV_b", [128, 16 * 2048], BF16)
    dint("wgP_b", [128, 16 * 2048], BF16); dint("woP_b", [128, 16 * 2048], BF16)
    dint("w1P_b", [128, 64 * 2048], BF16); dint("w3P_b", [128, 64 * 2048], BF16)
    dint("at_b", [HF, DM], BF16); dint("w2ht_b", [128, 4 * 64 * 512], BF16)
    dint("x1d", [DM, WIN], F32); dint("x2d", [DM, WIN], F32); dint("x3d", [DM, CH], F32)
    dint("xn2d", [DM, WIN], BF16)
    dint("qtmd", [CH, DM], BF16)
    outy = nc.dram_tensor("outy", [DM, CH], F32, kind="ExternalOutput").ap()

    sem_names = ["s_in", "s_st", "s_ws", "s_sp", "s_pe", "s_v", "s_a", "s_out",
                 "s_x", "s_fx", "s_rp", "s_wv", "s_qt",
                 "s_bw0", "s_bw1", "sp0", "sp1", "sp2",
                 "pA0", "pA1", "pA2", "pB0", "pB1", "pC0", "pC1", "pC2",
                 "pG0", "pG1", "pG2"]

    with contextlib.ExitStack() as stk:
        def sb(name, shape, dt):
            return stk.enter_context(nc.sbuf_tensor(name, shape, dt))

        cidn = sb("cidn", [128, 128], BF16)
        ch128 = sb("ch128", [128, 128], BF16)
        ch64 = sb("ch64", [64, 64], BF16)
        conesb = sb("conesb", [128, 128], BF16)
        cones1 = sb("cones1", [1, 128], F32)
        cnormrow = sb("cnormrow", [1, DM], BF16)
        rstdb = sb("rstdb", [1, 512], BF16)
        cqsc = sb("cqsc", [128, 128], F32)
        cksc = sb("cksc", [128, 128], F32)
        ciotakd = sb("ciotakd", [128, 2], F32)
        cslope = sb("cslope", [1, 16], F32)
        nsrep = sb("nsrep", [128, 16], F32)
        bdrep = sb("bdrep", [128, 16], F32)
        kdect = sb("kdect", [128, NH, 2], F32)
        xf32 = sb("xf32", [128, 8192], F32)
        lnbf = sb("lnbf", [128, 2, 512], BF16)
        lnsq = sb("lnsq", [128, 2, 512], BF16)
        strow = sb("strow", [1, 2048], F32)
        repst = sb("repst", [128, 2, 512], F32)
        ftmp = sb("ftmp", [128, 2, 512], F32)
        fx2 = sb("fx2", [128, 1, 512], F32)
        wbufA = sb("wbufA", [128, 3, 2048], BF16)
        wbufB = sb("wbufB", [128, 2, 2048], BF16)
        silb = sb("silb", [128, 3, 512], BF16)
        ropeb = sb("ropeb", [128, 2, 2, 128], BF16)
        inv1 = sb("inv1", [1, 512], F32)
        arena = sb("arena", [128, 61440], BF16)
        invr = arena[:, 60416:61440].bitcast(F32)

        psn = [stk.enter_context(nc.psum_tensor(f"ps{i}", [128, 512], F32))
               for i in range(8)]
        sems = {s: stk.enter_context(nc.semaphore(s)) for s in sem_names}
        block = stk.enter_context(nc.Block())

        def av(off, n):
            return arena[:, off:off + n]

        XN = [av(0 + k * 512, 512) for k in range(16)]
        QF = [av(8192 + k * 512, 512) for k in range(16)]
        KF = [av(16384 + k * 512, 512) for k in range(16)]
        VT = [av(24576 + t * 2048, 2048) for t in range(4)]
        AO = [av(32768 + k * 512, 512) for k in range(16)]
        QKD = [[av(40960 + (p_ * 2 + s_) * 256, 256) for s_ in range(2)] for p_ in range(2)]
        QD2 = [av(41984 + p_ * 256, 256) for p_ in range(2)]
        KD2 = [[av(42496 + (p_ * 2 + s_) * 128, 128) for s_ in range(2)] for p_ in range(2)]
        GT = QF
        PR = KF
        XN2O = QF
        HID = [av(0 + h * 512, 512) for h in range(64)]
        XNF = [av(32768 + k * 512, 512) for k in range(16)]
        KFM = [av(0 + h * 3072, 3072) for h in range(4)]
        VTM = [av(12288 + t * 512, 512) for t in range(24)]
        XN3 = [av(24576 + k * 512, 512) for k in range(16)]
        KTM = av(32768, 512)
        KNR = av(33280, 512)
        QTM4 = [av(33792 + t * 2048, 2048) for t in range(4)]
        RSC1 = av(41984, 2048)
        RSC2 = av(44032, 2048)
        WVB = av(46080, 8192)
        QFMq = [av(33792 + h * 512, 512) for h in range(16)]
        EXPB = [av(41984 + i * 512, 512) for i in range(4)]
        AN = [av(44032 + k * 512, 512) for k in range(16)]
        BM = [av(52224 + i * 512, 512) for i in range(N_APPLY)]
        QTMR = av(58368, 2048)
        dmaskt = av(43008, 8192).rearrange("p (h s m) -> p h s m", s=2, m=256)
        qdect = av(51200, 4096).rearrange("p (h m) -> p h m", m=256)
        kvf = arena[:, 55296:55296 + 4096].bitcast(F32).rearrange("p (h e) -> p h e", e=128)
        kvb = av(59392, 2048).rearrange("p (h e) -> p h e", e=128)
        ciotaqd = arena[:, 55296:55296 + 512].bitcast(F32)
        ciotadT = arena[:, 55808:55808 + 1024].bitcast(F32).rearrange("p (s m) -> p s m", m=256)
        ctri = arena[:, 56832:56832 + 1024].bitcast(F32).rearrange("p (s m) -> p s m", m=256)

        XH = [xf32[:, k * 512:(k + 1) * 512] for k in range(16)]

        first_counts = {}

        def program(E: Emit):
            ps = psn

            E.dma(Y, cidn[:], D["idn"][:, :], "s_in")
            E.dma(Y, ch128[:], D["h128"][:, :], "s_in")
            E.dma(Y, ch64[:], D["h64"][:, :], "s_in")
            E.dma(Y, conesb[:], D["onesb"][:, :], "s_in")
            E.dma(Y, cones1[:], D["ones1x128"][:, :], "s_in")
            E.dma(G, cnormrow[:], D["normw_row"][:, :], "s_st")
            E.dma(Y, cqsc[:], D["qscale_rep"][:, :], "s_in")
            E.dma(Y, cksc[:], D["kscale_rep"][:, :], "s_in")
            E.dma(Y, ciotaqd[:], D["iotaqd"][:, :], "s_in")
            E.dma(Y, ciotakd[:], D["iotakd"][:, :], "s_in")
            E.dma(Y, ciotadT[:], D["iotadT"][:, :, :], "s_in")
            E.dma(Y, ctri[:], D["tri"][:, :, :], "s_in")
            E.dma(Y, cslope[:], D["slope_r"][:, :], "s_in")
            E.wait(T, "s_in")
            E.op(T, lambda t: t.matmul(ps[0][:, 0:16], cones1[:], cslope[:],
                                       start=True, stop=True), "s_pe")
            E.wait(V_, "s_pe")
            E.op(V_, lambda v: v.tensor_scalar_mul(nsrep[:], ps[0][:, 0:16], -1.0), "s_v")
            E.rel("ps0", "s_v")
            E.wait(S, "s_v")
            E.op(S, lambda s: s.activation(bdrep[:], nsrep[:], AF.Exp, scale=256.0), "s_a")
            for h in range(NH):
                E.op(S, lambda s, h=h: s.activation(
                    qdect[:, h, :], ciotaqd[:], AF.Exp, scale=nsrep[:, h:h + 1]), "s_a")
                E.op(S, lambda s, h=h: s.activation(
                    kdect[:, h, :], ciotakd[:], AF.Exp, scale=nsrep[:, h:h + 1]), "s_a")
                for su in range(2):
                    E.drain(S)
                    E.op(S, lambda s, h=h, su=su: s.activation(
                        ftmp[:, su, 0:256], ciotadT[:, su, :], AF.Exp,
                        scale=nsrep[:, h:h + 1]), "s_a")
                    E.wait(V_, "s_a")
                    E.drain(V_)
                    E.op(V_, lambda v, h=h, su=su: v.tensor_mul(
                        dmaskt[:, h, su, :], ftmp[:, su, 0:256], ctri[:, su, :]), "s_v")
                    E.wait(S, "s_v")
            E.drain(V_)
            E.op(V_, lambda v: v.memset(kvf[:], 0.0), "s_v")
            E.wait(S, "s_v")
            E.op(S, lambda s: s.copy(kvb[:], kvf[:]), "s_a")

            a_marks = []
            for dblk in range(4):
                for cg in range(64):
                    it = dblk * 64 + cg
                    rot = it % 3
                    qq = G
                    buf = wbufA[:, rot, 0:512]
                    if it >= 3:
                        E.wait(qq, "s_pe", a_marks[it - 3])
                    E.dma(qq, buf, D["w2T"][cg * 128:(cg + 1) * 128,
                                            dblk * 512:(dblk + 1) * 512], f"pG{rot}")
                    E.wait(T, f"pG{rot}")
                    E.acq(f"ps{rot}")
                    E.op(T, lambda t, rot=rot, buf=buf: t.matmul(
                        ps[rot][:], ch128[:], buf, start=True, stop=True), "s_pe")
                    a_marks.append(E.cnt["s_pe"])
                    E.wait(S, "s_pe")
                    E.wait(S, f"sp{rot}")
                    E.drain(S)
                    E.op(S, lambda s, rot=rot: s.activation(
                        silb[:, rot, :], ps[rot][:], AF.Copy), "s_a")
                    E.rel(f"ps{rot}", "s_a")
                    E.wait(Y, "s_a")
                    E.dma(Y, D["at_b"][cg * 128:(cg + 1) * 128,
                                       dblk * 512:(dblk + 1) * 512],
                          silb[:, rot, :], f"sp{rot}")
            passa_sp = [E.cnt["sp0"], E.cnt["sp1"], E.cnt["sp2"]]
            atr = D["at_b"].rearrange("(cg p) d -> cg p d", p=128)
            for w in (Y, G, S):
                for si in range(3):
                    E.wait(w, f"sp{si}", passa_sp[si])
            w2out = D["w2ht_b"].rearrange("p (db hh d) -> p db hh d", db=4, d=512)
            PB = [[av((par * 4 + db) * 512, 512) for db in range(4)] for par in range(2)]
            b_marks = []
            for rj in range(128):
                par = rj % 2
                qq = (G, Y)[par]
                buf = wbufB[:, par, :][0:64, :]
                if rj >= 2:
                    E.wait(qq, "s_pe", b_marks[4 * (rj - 2) + 3])
                E.dma(qq, buf, atr[:, rj, :], f"pA{par}")
                E.wait(T, f"pA{par}")
                for db in range(4):
                    E.acq(f"ps{par * 4 + db}")
                    E.op(T, lambda t, par=par, db=db, buf=buf: t.matmul(
                        ps[par * 4 + db][0:64, :], ch64[:],
                        buf[:, db * 512:(db + 1) * 512], start=True, stop=True), "s_pe")
                    b_marks.append(E.cnt["s_pe"])
                E.wait(S, "s_pe")
                E.wait(S, f"s_bw{par}")
                for db in range(4):
                    E.drain(S)
                    E.op(S, lambda s, par=par, db=db: s.activation(
                        PB[par][db][0:64, :], ps[par * 4 + db][0:64, :], AF.Copy,
                        scale=FW_SCALE), "s_a")
                    E.rel(f"ps{par * 4 + db}", "s_a")
                E.wait(S, "s_a")
                for db in range(4):
                    E.dma(S, w2out[rj, db, :, :], PB[par][db][0:64, :], f"s_bw{par}")
            stage_mark = {}
            for nm, bnm, ncols in (("wqkvP", "wqkvP_b", 32 * 2048),
                                   ("wqkvV", "wqkvV_b", 16 * 2048),
                                   ("wgP", "wgP_b", 16 * 2048),
                                   ("woP", "woP_b", 16 * 2048),
                                   ("w1P", "w1P_b", 64 * 2048),
                                   ("w3P", "w3P_b", 64 * 2048)):
                nsl = max(4, ncols // 16384)
                for sl in range(nsl):
                    c0, c1 = sl * (ncols // nsl), (sl + 1) * (ncols // nsl)
                    E.dma(G, D[bnm][:, c0:c1], D[nm][:, c0:c1], "s_st")
                stage_mark[nm] = E.cnt["s_st"]
            stage_done_w = E.cnt["s_st"]

            w2ht_bw0 = E.cnt["s_bw0"]
            w2ht_bw1 = E.cnt["s_bw1"]
            for w in ENGS:
                E.wait(w, "s_bw0", w2ht_bw0)
                E.wait(w, "s_bw1", w2ht_bw1)
            E.barrier()

            def ln_fm(x_tiles, xn_tiles, eps):
                E.wait(S, "s_v")
                pe_marks = []
                for k in range(16):
                    if k >= 2:
                        E.wait(V_, "s_pe", pe_marks[k - 2])
                        E.wait(S, "s_pe", pe_marks[k - 2])
                    E.drain(V_)
                    E.op(V_, lambda v, k=k: v.tensor_copy(lnbf[:, k % 2, :], x_tiles[k]), "s_v")
                    E.drain(S)
                    E.op(S, lambda s, k=k: s.activation(
                        lnsq[:, k % 2, :], x_tiles[k], AF.Square), "s_a")
                    E.wait(T, "s_v")
                    E.wait(T, "s_a")
                    if k == 0:
                        E.acq("ps6"); E.acq("ps7")
                    E.op(T, lambda t, k=k: t.matmul(
                        ps[6][0:1, :], conesb[:, 0:1], lnbf[:, k % 2, :],
                        start=(k == 0), stop=(k == 15)), "s_pe")
                    E.op(T, lambda t, k=k: t.matmul(
                        ps[7][0:1, :], conesb[:, 0:1], lnsq[:, k % 2, :],
                        start=(k == 0), stop=(k == 15)), "s_pe")
                    pe_marks.append(E.cnt["s_pe"])
                E.wait(V_, "s_pe")
                E.drain(V_)
                E.op(V_, lambda v: v.tensor_copy(strow[:, 0:512], ps[6][0:1, :]), "s_v")
                E.op(V_, lambda v: v.tensor_copy(strow[:, 512:1024], ps[7][0:1, :]), "s_v")
                E.rel("ps6", "s_v"); E.rel("ps7", "s_v")
                E.drain(V_)
                E.op(V_, lambda v: v.tensor_scalar_mul(strow[:, 0:512], strow[:, 0:512], 1.0 / DM), "s_v")
                E.op(V_, lambda v: v.tensor_scalar_mul(strow[:, 512:1024], strow[:, 512:1024], 1.0 / DM), "s_v")
                E.drain(V_)
                E.op(V_, lambda v: v.tensor_mul(strow[:, 1024:1536], strow[:, 0:512], strow[:, 0:512]), "s_v")
                E.drain(V_)
                E.op(V_, lambda v: v.tensor_sub(strow[:, 1024:1536], strow[:, 512:1024], strow[:, 1024:1536]), "s_v")
                E.drain(V_)
                E.op(V_, lambda v: v.tensor_scalar_add(strow[:, 1024:1536], strow[:, 1024:1536], float(eps)), "s_v")
                E.wait(S, "s_v")
                E.drain(S)
                E.op(S, lambda s: s.activation(strow[:, 1536:2048], strow[:, 1024:1536], AF.Sqrt), "s_a")
                E.wait(V_, "s_a")
                E.drain(V_)
                E.op(V_, lambda v: v.reciprocal(strow[:, 1024:1536], strow[:, 1536:2048]), "s_v")
                E.drain(V_)
                E.op(V_, lambda v: v.tensor_mul(strow[:, 512:1024], strow[:, 0:512], strow[:, 1024:1536]), "s_v")
                E.wait(T, "s_v")
                E.acq("ps6"); E.acq("ps7")
                E.op(T, lambda t: t.matmul(ps[6][:], cones1[:], strow[:, 1024:1536],
                                           start=True, stop=True), "s_pe")
                E.op(T, lambda t: t.matmul(ps[7][:], cones1[:], strow[:, 512:1024],
                                           start=True, stop=True), "s_pe")
                E.wait(V_, "s_pe")
                E.drain(V_)
                E.op(V_, lambda v: v.tensor_copy(repst[:, 0, :], ps[6][:]), "s_v")
                E.op(V_, lambda v: v.tensor_copy(repst[:, 1, :], ps[7][:]), "s_v")
                E.rel("ps6", "s_v"); E.rel("ps7", "s_v")
                for k in range(16):
                    E.drain(V_)
                    E.op(V_, lambda v, k=k: v.tensor_mul(ftmp[:, k % 2, :], x_tiles[k], repst[:, 0, :]), "s_v")
                    E.drain(V_)
                    E.op(V_, lambda v, k=k: v.tensor_sub(xn_tiles[k], ftmp[:, k % 2, :], repst[:, 1, :]), "s_v")

            def proj_c1(wdram, blk0, rhs_tiles, evac, st_mark=0):
                marks = []
                E.wait(Y, "s_pe")
                E.wait(Y, "s_st", st_mark)
                for oc in range(16):
                    rot = oc % 3
                    buf = wbufA[:, rot, :]
                    if oc >= 3:
                        E.wait(Y, "s_pe", marks[oc - 3])
                    j = blk0 + oc
                    E.dma(Y, buf, wdram[:, j * 2048:(j + 1) * 2048], f"pA{rot}")
                    E.wait(T, f"pA{rot}")
                    pb = f"ps{4 + rot}"
                    E.acq(pb)
                    for i in range(16):
                        E.op(T, lambda t, i=i, rot=rot, buf=buf: t.matmul(
                            ps[4 + rot][:],
                            buf.rearrange("p (i o) -> p i o", o=128)[:, i, :],
                            rhs_tiles[i], start=(i == 0), stop=(i == 15)),
                            "s_pe" if i == 15 else None)
                    marks.append(E.cnt["s_pe"])
                    evac(oc, 4 + rot)

            for si in range(5):
                prefix = si < 2
                for hf in range(2):
                    if si < 2:
                        base, coff = D["xp_fm"], si * CH + hf * 512
                    else:
                        base, coff = D["xw_fm"], (si - 2) * CH + hf * 512
                    wcol = (si - 2) * CH + hf * 512

                    E.barrier()
                    E.dma(Y, xf32[:].rearrange("p (k t) -> p k t", t=512),
                          base.rearrange("(k p) t -> p k t", p=128)[:, :, coff:coff + 512],
                          "s_x")
                    E.wait(V_, "s_x"); E.wait(S, "s_x"); E.wait(T, "s_x")
                    ln_fm(XH, XN, EPS_LN)
                    E.wait(T, "s_v")

                    def ev_k(oc, pi):
                        E.wait(S, "s_pe")
                        E.drain(S)
                        E.op(S, lambda s, oc=oc, pi=pi: s.activation(
                            KF[oc], ps[pi][:], AF.Silu), "s_a")
                        E.rel(f"ps{pi}", "s_a")
                    proj_c1(D["wqkvP_b"], 16, XN, ev_k, stage_mark["wqkvP"])

                    vpws = [arena[:, 32768:40960].rearrange("p (i o) -> p i o", o=512),
                            arena[:, 8192:16384].rearrange("p (i o) -> p i o", o=512)]
                    E.wait(S, "s_st", stage_mark["wqkvV"])
                    vmarks = [0, 0, 0, 0]
                    ws_marks = [0, 0, 0, 0]
                    E.wait(S, "s_pe")
                    E.dma(S, vpws[0], D["wqkvV_b"][:, 0:8192], "s_ws")
                    ws_marks[0] = E.cnt["s_ws"]
                    for vc in range(4):
                        if vc + 1 < 4:
                            if vc >= 1:
                                E.wait(S, "s_pe", vmarks[vc - 1])
                            E.dma(S, vpws[(vc + 1) % 2],
                                  D["wqkvV_b"][:, (vc + 1) * 8192:(vc + 2) * 8192], "s_ws")
                            ws_marks[vc + 1] = E.cnt["s_ws"]
                        E.wait(T, "s_ws", ws_marks[vc])
                        for tb in range(4):
                            pb = f"ps{4 + (tb % 2)}"
                            E.acq(pb)
                            for i in range(16):
                                E.op(T, lambda t, i=i, tb=tb, vc=vc: t.matmul(
                                    ps[4 + (tb % 2)][:],
                                    XN[i][:, tb * 128:(tb + 1) * 128],
                                    vpws[vc % 2][:, i, :], start=(i == 0), stop=(i == 15)),
                                    "s_pe" if i == 15 else None)
                            E.wait(S, "s_pe")
                            E.drain(S)
                            E.op(S, lambda s, tb=tb, vc=vc: s.activation(
                                VT[tb][:, vc * 512:(vc + 1) * 512],
                                ps[4 + (tb % 2)][:], AF.Silu), "s_a")
                            E.rel(f"ps{4 + (tb % 2)}", "s_a")
                        vmarks[vc] = E.cnt["s_pe"]

                    if not prefix:
                        def ev_q(oc, pi):
                            E.wait(S, "s_pe")
                            E.drain(S)
                            E.op(S, lambda s, oc=oc, pi=pi: s.activation(
                                QF[oc], ps[pi][:], AF.Silu), "s_a")
                            E.rel(f"ps{pi}", "s_a")
                        proj_c1(D["wqkvP_b"], 0, XN, ev_q, stage_mark["wqkvP"])

                    last_qkd_pe = [0, 0]
                    last_kd_pe = [0, 0]
                    for blk in range(2):
                        tb0 = blk * 2
                        E.wait(T, "s_a")
                        for h in range(NH):
                            par = h % 2
                            if not prefix:
                                for su in range(2):
                                    E.acq(f"ps{su}")
                                    E.op(T, lambda t, h=h, su=su, blk=blk: t.matmul(
                                        ps[su][:, 0:256],
                                        KF[h][:, blk * 256 + su * 128: blk * 256 + (su + 1) * 128],
                                        QF[h][:, blk * 256:(blk + 1) * 256],
                                        start=True, stop=True), "s_pe")
                                qk_pe = E.cnt["s_pe"]
                                E.wait(V_, "s_pe", max(qk_pe, last_qkd_pe[par]))
                                E.drain(V_)
                                for su in range(2):
                                    E.op(V_, lambda v, h=h, su=su, par=par: v.tensor_mul(
                                        QKD[par][su], ps[su][:, 0:256], dmaskt[:, h, su, :]), "s_v")
                                    E.rel(f"ps{su}", "s_v")
                                E.op(V_, lambda v, h=h, blk=blk, par=par: v.tensor_mul(
                                    QD2[par], QF[h][:, blk * 256:(blk + 1) * 256],
                                    qdect[:, h, :]), "s_v")
                                qkd_v = E.cnt["s_v"]
                            for su in range(2):
                                E.acq(f"ps{2 + su}")
                                E.op(T, lambda t, h=h, su=su, blk=blk: t.matmul(
                                    ps[2 + su][:, 0:128],
                                    KF[h][:, blk * 256 + su * 128: blk * 256 + (su + 1) * 128],
                                    cidn[:], start=True, stop=True), "s_pe")
                            kd_pe = E.cnt["s_pe"]
                            E.wait(S, "s_pe", max(kd_pe, last_kd_pe[par]))
                            for su in range(2):
                                E.drain(S)
                                E.op(S, lambda s, h=h, su=su, par=par: s.activation(
                                    KD2[par][su], ps[2 + su][:, 0:128], AF.Copy,
                                    scale=kdect[:, h, su:su + 1]), "s_a")
                                E.rel(f"ps{2 + su}", "s_a")
                            kd_a = E.cnt["s_a"]

                            if not prefix:
                                E.wait(T, "s_v", qkd_v)
                                E.acq("ps6")
                                for su in range(2):
                                    E.op(T, lambda t, h=h, su=su, tb0=tb0, par=par: t.matmul(
                                        ps[6][:, 0:256],
                                        VT[tb0 + su][:, h * 128:(h + 1) * 128],
                                        QKD[par][su], start=(su == 0), stop=False), None)
                                E.op(T, lambda t, h=h, par=par: t.matmul(
                                    ps[6][:, 0:256], kvb[:, h, :], QD2[par],
                                    start=False, stop=True), "s_pe")
                                last_qkd_pe[par] = E.cnt["s_pe"]
                                E.wait(S, "s_pe")
                                E.drain(S)
                                E.op(S, lambda s, h=h, blk=blk: s.activation(
                                    AO[h][:, blk * 256:(blk + 1) * 256],
                                    ps[6][:, 0:256], AF.Copy), "s_a")
                                E.rel("ps6", "s_a")

                            E.wait(T, "s_a", kd_a)
                            E.acq("ps7")
                            for su in range(2):
                                E.op(T, lambda t, h=h, su=su, tb0=tb0, par=par: t.matmul(
                                    ps[7][:, 0:128], KD2[par][su],
                                    VT[tb0 + su][:, h * 128:(h + 1) * 128],
                                    start=(su == 0), stop=(su == 1)),
                                    "s_pe" if su == 1 else None)
                            last_kd_pe[par] = E.cnt["s_pe"]
                            E.wait(V_, "s_pe")
                            E.drain(V_)
                            E.op(V_, lambda v, h=h: v.scalar_tensor_tensor(
                                kvf[:, h, :], kvf[:, h, :], bdrep[:, h:h + 1],
                                ps[7][:, 0:128], OP.mult, OP.add), "s_v")
                            E.rel("ps7", "s_v")
                            E.wait(S, "s_v")
                            E.drain(S)
                            E.op(S, lambda s, h=h: s.copy(kvb[:, h, :], kvf[:, h, :]), "s_a")

                    if prefix:
                        continue

                    def ev_g(oc, pi):
                        E.wait(S, "s_pe")
                        E.drain(S)
                        E.op(S, lambda s, oc=oc, pi=pi: s.activation(
                            GT[oc], ps[pi][:], AF.Sigmoid), "s_a")
                        E.rel(f"ps{pi}", "s_a")
                    proj_c1(D["wgP_b"], 0, XN, ev_g, stage_mark["wgP"])

                    pe_marks = []
                    for k in range(16):
                        if k >= 2:
                            E.wait(S, "s_pe", pe_marks[k - 2])
                        E.drain(S)
                        E.op(S, lambda s, k=k: s.activation(
                            lnsq[:, k % 2, :], AO[k], AF.Square), "s_a")
                        E.wait(T, "s_a")
                        if k == 0:
                            E.acq("ps6")
                        E.op(T, lambda t, k=k: t.matmul(
                            ps[6][0:1, :], conesb[:, 0:1], lnsq[:, k % 2, :],
                            start=(k == 0), stop=(k == 15)), "s_pe")
                        pe_marks.append(E.cnt["s_pe"])
                    E.wait(V_, "s_pe")
                    E.drain(V_)
                    E.op(V_, lambda v: v.tensor_scalar_mul(strow[:, 0:512], ps[6][0:1, :], 1.0 / DM), "s_v")
                    E.rel("ps6", "s_v")
                    E.drain(V_)
                    E.op(V_, lambda v: v.tensor_scalar_add(strow[:, 0:512], strow[:, 0:512], EPS_A), "s_v")
                    E.wait(S, "s_v")
                    E.drain(S)
                    E.op(S, lambda s: s.activation(strow[:, 512:1024], strow[:, 0:512], AF.Sqrt), "s_a")
                    E.wait(V_, "s_a")
                    E.drain(V_)
                    E.op(V_, lambda v: v.reciprocal(strow[:, 1024:1536], strow[:, 512:1024]), "s_v")
                    E.drain(V_)
                    E.op(V_, lambda v: v.tensor_copy(rstdb[:], strow[:, 1024:1536]), "s_v")
                    rms_v = E.cnt["s_v"]
                    E.wait(T, "s_v", rms_v)
                    E.wait(T, "s_st", 16)
                    for k in range(16):
                        pb = f"ps{4 + (k % 2)}"
                        E.acq(pb)
                        E.op(T, lambda t, k=k: t.matmul(
                            ps[4 + (k % 2)][:], cnormrow[:, k * 128:(k + 1) * 128],
                            rstdb[:], start=True, stop=True), "s_pe")
                        E.wait(V_, "s_pe")
                        E.drain(V_)
                        E.op(V_, lambda v, k=k: v.tensor_mul(
                            silb[:, k % 2, :], AO[k], ps[4 + (k % 2)][:]), "s_v")
                        E.rel(pb, "s_v")
                        E.drain(V_)
                        E.op(V_, lambda v, k=k: v.tensor_mul(PR[k], silb[:, k % 2, :], GT[k]), "s_v")
                    E.wait(T, "s_v")

                    def ev_o(oc, pi):
                        E.wait(V_, "s_pe")
                        E.drain(V_)
                        E.op(V_, lambda v, oc=oc, pi=pi: v.tensor_add(
                            XH[oc], XH[oc], ps[pi][:]), "s_v")
                        E.rel(f"ps{pi}", "s_v")
                    proj_c1(D["woP_b"], 0, PR, ev_o, stage_mark["woP"])
                    x1_v = E.cnt["s_v"]

                    E.wait(Y, "s_v", x1_v)
                    E.dma(Y, D["x1d"].rearrange("(k p) t -> p k t", p=128)[:, :, wcol:wcol + 512],
                          xf32[:].rearrange("p (k t) -> p k t", t=512), "s_sp")
                    ln_fm(XH, XN2O, EPS_LN)
                    E.wait(Y, "s_v")
                    E.dma(Y, D["xn2d"].rearrange("(k p) t -> p k t", p=128)[:, :, wcol:wcol + 512],
                          arena[:, 8192:16384].rearrange("p (k t) -> p k t", t=512), "s_sp")

            if stop == "x1":
                E.barrier()
                E.wait(Y, "s_sp")
                E.dma(Y, outy[:, :], D["x1d"][:, CH:2 * CH], "s_out")
                E.wait(Y, "s_out")
                return

            w1r = D["w1P_b"]
            w3r = D["w3P_b"]
            w2hr = D["w2ht_b"].rearrange("p (db hh d) -> p db hh d", db=4, d=512)

            def ffn_sweep(xn_tiles, res_dram, res_col, out_dram, out_col, store_sem):
                pe_w = []
                E.wait(G, "s_st", stage_mark["w1P"])
                E.wait(Y, "s_st", stage_mark["w1P"])
                E.wait(S, "s_st", stage_mark["w3P"])
                for hc in range(64):
                    rot = hc % 3
                    rotb = hc % 2
                    qq = (G, Y)[hc % 2]
                    bufa = wbufA[:, rot, :]
                    bufb = wbufB[:, rotb, :]
                    if hc >= 3:
                        E.wait(qq, "s_pe", pe_w[2 * (hc - 3)])
                    if hc >= 2:
                        E.wait(S, "s_pe", pe_w[2 * (hc - 2) + 1])
                    E.dma(qq, bufa, w1r[:, hc * 2048:(hc + 1) * 2048], f"pA{rot}")
                    E.dma(S, bufb, w3r[:, hc * 2048:(hc + 1) * 2048], f"pC{rotb}")
                    E.wait(T, f"pA{rot}"); E.wait(T, f"pC{rotb}")
                    pa, pb = f"ps{rot}", f"ps{3 + rotb}"
                    E.acq(pa); E.acq(pb)
                    for i in range(16):
                        E.op(T, lambda t, i=i, rot=rot, bufa=bufa: t.matmul(
                            ps[rot][:], bufa.rearrange("p (i o) -> p i o", o=128)[:, i, :],
                            xn_tiles[i], start=(i == 0), stop=(i == 15)),
                            "s_pe" if i == 15 else None)
                    pe_w.append(E.cnt["s_pe"])
                    for i in range(16):
                        E.op(T, lambda t, i=i, rotb=rotb, bufb=bufb: t.matmul(
                            ps[3 + rotb][:], bufb.rearrange("p (i o) -> p i o", o=128)[:, i, :],
                            xn_tiles[i], start=(i == 0), stop=(i == 15)),
                            "s_pe" if i == 15 else None)
                    pe_w.append(E.cnt["s_pe"])
                    E.wait(S, "s_pe", pe_w[-2])
                    E.wait(S, "s_v")
                    E.drain(S)
                    E.op(S, lambda s, rot=rot: s.activation(
                        silb[:, rot, :], ps[rot][:], AF.Silu), "s_a")
                    E.rel(pa, "s_a")
                    E.wait(V_, "s_a")
                    E.wait(V_, "s_pe", pe_w[-1])
                    E.drain(V_)
                    E.op(V_, lambda v, hc=hc, rot=rot, rotb=rotb: v.tensor_mul(
                        HID[hc], silb[:, rot, :], ps[3 + rotb][:]), "s_v")
                    E.rel(pb, "s_v")
                E.wait(T, "s_v")
                E.wait(Y, "s_pe")
                E.wait(G, "s_pe")
                h4_marks = []
                for dblk in range(4):
                    for pi in range(4):
                        E.acq(f"ps{pi}")
                    for h4 in range(16):
                        it4 = dblk * 16 + h4
                        rot = it4 % 3
                        qq2 = (Y, G)[it4 % 2]
                        buf = wbufA[:, rot, :]
                        if it4 >= 3:
                            E.wait(qq2, "s_pe", h4_marks[it4 - 3])
                        E.dma(qq2, buf[:, 0:2048].rearrange("p (hh d) -> p hh d", d=512),
                              w2hr[:, dblk, h4 * 4:(h4 + 1) * 4, :], f"pA{rot}")
                        E.wait(T, f"pA{rot}")
                        for hh in range(4):
                            h = h4 * 4 + hh
                            for dsub in range(4):
                                E.op(T, lambda t, h=h, hh=hh, dsub=dsub, buf=buf: t.matmul(
                                    ps[dsub][:],
                                    buf.rearrange("p (hh d) -> p hh d", d=512)[:, hh, dsub * 128:(dsub + 1) * 128],
                                    HID[h], start=(h == 0), stop=(h == 63)),
                                    "s_pe" if dsub == 3 else None)
                        h4_marks.append(E.cnt["s_pe"])
                    w2_pe = E.cnt["s_pe"]
                    for dsub in range(4):
                        d = dblk * 4 + dsub
                        E.wait(Y, "s_v")
                        E.dma(Y, fx2[:, 0, :],
                              res_dram[d * 128:(d + 1) * 128, res_col:res_col + 512], "s_fx")
                        E.wait(V_, "s_pe", w2_pe)
                        E.wait(V_, "s_fx")
                        E.wait(V_, store_sem)
                        E.drain(V_)
                        E.op(V_, lambda v, dsub=dsub: v.tensor_add(
                            ftmp[:, dsub % 2, :], ps[dsub][:], fx2[:, 0, :]), "s_v")
                        E.rel(f"ps{dsub}", "s_v")
                        E.wait(Y, "s_v")
                        E.dma(Y, out_dram[d * 128:(d + 1) * 128, out_col:out_col + 512],
                              ftmp[:, dsub % 2, :], store_sem)

            E.wait(Y, "s_bw0", w2ht_bw0)
            E.wait(Y, "s_bw1", w2ht_bw1)
            for g in range(WIN // FG):
                col = g * FG
                E.barrier()
                E.dma(Y, arena[:, 32768:40960].rearrange("p (k t) -> p k t", t=512),
                      D["xn2d"].rearrange("(k p) t -> p k t", p=128)[:, :, col:col + 512], "s_x")
                E.wait(T, "s_x"); E.wait(V_, "s_x"); E.wait(S, "s_x")
                ffn_sweep(XNF, D["x1d"], col, D["x2d"], col, "s_sp")

            if stop == "x2":
                E.barrier()
                E.wait(Y, "s_sp")
                E.dma(Y, outy[:, :], D["x2d"][:, CH:2 * CH], "s_out")
                E.wait(Y, "s_out")
                return

            def rope_norm(dst, heads, wt0, scale_rep):
                rot = (wt0 // 128) % 2
                E.wait(G, "s_v")
                E.dma(G, ropeb[:, rot, 0, :], D["ropecos"][wt0:wt0 + 128, :], "s_rp")
                E.dma(G, ropeb[:, rot, 1, :], D["ropesinF"][wt0:wt0 + 128, :], "s_rp")
                E.wait(V_, "s_rp")
                W = heads * 128
                dv = dst.rearrange("p (h j) -> p h j", j=128)
                cosv = ropeb[:, rot, 0, :].unsqueeze(1).broadcast_to([128, heads, 128])
                sinL = ropeb[:, rot, 1, 0:64].unsqueeze(1).broadcast_to([128, heads, 64])
                sinH = ropeb[:, rot, 1, 64:128].unsqueeze(1).broadcast_to([128, heads, 64])
                t1 = RSC1[:, 0:W].rearrange("p (h j) -> p h j", j=128)
                t2 = RSC2[:, 0:W].rearrange("p (h j) -> p h j", j=128)
                E.drain(V_)
                E.op(V_, lambda v: v.tensor_mul(t1[:, :, 0:64], dv[:, :, 64:128], sinL), "s_v")
                E.op(V_, lambda v: v.tensor_mul(t1[:, :, 64:128], dv[:, :, 0:64], sinH), "s_v")
                E.drain(V_)
                E.op(V_, lambda v: v.tensor_mul(t2, dv, cosv), "s_v")
                E.drain(V_)
                E.op(V_, lambda v: v.tensor_add(dv, t1, t2), "s_v")
                E.wait(S, "s_v")
                E.drain(S)
                E.op(S, lambda s: s.activation(RSC1[:, 0:W], dst, AF.Square), "s_a")
                E.wait(V_, "s_a")
                E.drain(V_)
                sq = RSC1[:, 0:W].rearrange("p (h j) -> p h j", j=128)
                E.op(V_, lambda v: v.reduce_sum(
                    ftmp[:, 0, 0:heads].rearrange("p (h o) -> p h o", o=1), sq,
                    axis=mybir.AxisListType.X), "s_v")
                E.wait(S, "s_v")
                E.drain(S)
                E.op(S, lambda s: s.activation(ftmp[:, 0, 16:16 + heads], ftmp[:, 0, 0:heads], AF.Sqrt), "s_a")
                E.wait(V_, "s_a")
                E.drain(V_)
                E.op(V_, lambda v: v.tensor_scalar_add(ftmp[:, 0, 32:32 + heads], ftmp[:, 0, 16:16 + heads], EPS_A), "s_v")
                E.drain(V_)
                E.op(V_, lambda v: v.reciprocal(ftmp[:, 0, 48:48 + heads], ftmp[:, 0, 32:32 + heads]), "s_v")
                E.drain(V_)
                invv = ftmp[:, 0, 48:48 + heads].rearrange("p (h o) -> p h o", o=1).broadcast_to([128, heads, 128])
                E.op(V_, lambda v: v.tensor_mul(dv, dv, invv), "s_v")
                E.drain(V_)
                scv = scale_rep[:, :].unsqueeze(1).broadcast_to([128, heads, 128])
                E.op(V_, lambda v: v.tensor_mul(dv, dv, scv), "s_v")

            wbkr = D["wbk"].rearrange("(i p) o -> p i o", p=128)
            wbvr = D["wbv"].rearrange("(i p) o -> p i o", p=128)
            wbqr = D["wbq"].rearrange("(i p) o -> p i o", p=128)
            wvbv = WVB.rearrange("p (i o) -> p i o", o=512)

            for g in range(6):
                col = g * 512
                E.barrier()
                E.wait(Y, "s_sp")
                E.dma(Y, xf32[:].rearrange("p (k t) -> p k t", t=512),
                      D["x2d"].rearrange("(k p) t -> p k t", p=128)[:, :, col:col + 512], "s_x")
                E.wait(V_, "s_x"); E.wait(S, "s_x"); E.wait(T, "s_x")
                ln_fm(XH, XN3, EPS_LN)
                E.wait(T, "s_v")

                E.wait(G, "s_pe")
                E.dma(G, wvbv, wbkr[:, :, :], "s_wv")
                E.wait(T, "s_wv")
                for tb in range(4):
                    pb = f"ps{4 + (tb % 2)}"
                    E.acq(pb)
                    for i in range(16):
                        E.op(T, lambda t, i=i, tb=tb: t.matmul(
                            ps[4 + (tb % 2)][:], XN3[i][:, tb * 128:(tb + 1) * 128],
                            wvbv[:, i, :], start=(i == 0), stop=(i == 15)),
                            "s_pe" if i == 15 else None)
                    E.wait(S, "s_pe")
                    E.drain(S)
                    E.op(S, lambda s: s.activation(KTM, ps[4 + (tb % 2)][:], AF.Copy), "s_a")
                    E.rel(pb, "s_a")
                    E.wait(V_, "s_a")
                    rope_norm(KTM, 4, col + tb * 128, cksc)
                    kn_v = E.cnt["s_v"]
                    E.wait(T, "s_v", kn_v)
                    for kvh in range(4):
                        pq = f"ps{kvh % 2}"
                        E.acq(pq)
                        E.op(T, lambda t, kvh=kvh: t.matmul(
                            ps[kvh % 2][:, 0:128], KTM[:, kvh * 128:(kvh + 1) * 128],
                            cidn[:], start=True, stop=True), "s_pe")
                        E.wait(S, "s_pe")
                        E.drain(S)
                        E.op(S, lambda s, kvh=kvh, g=g, tb=tb: s.activation(
                            KFM[kvh][:, col + tb * 128: col + tb * 128 + 128],
                            ps[kvh % 2][:, 0:128], AF.Copy), "s_a")
                        E.rel(pq, "s_a")

                E.wait(G, "s_a")
                E.dma(G, wvbv, wbvr[:, :, :], "s_wv")
                E.wait(T, "s_wv")
                for tb in range(4):
                    pb = f"ps{4 + (tb % 2)}"
                    E.acq(pb)
                    for i in range(16):
                        E.op(T, lambda t, i=i, tb=tb: t.matmul(
                            ps[4 + (tb % 2)][:], XN3[i][:, tb * 128:(tb + 1) * 128],
                            wvbv[:, i, :], start=(i == 0), stop=(i == 15)),
                            "s_pe" if i == 15 else None)
                    E.wait(S, "s_pe")
                    E.drain(S)
                    E.op(S, lambda s, g=g, tb=tb: s.activation(
                        VTM[g * 4 + tb], ps[4 + (tb % 2)][:], AF.Copy), "s_a")
                    E.rel(pb, "s_a")

                if g in (2, 3):
                    for cb in range(4):
                        E.wait(G, "s_a")
                        E.dma(G, wvbv, wbqr[:, :, cb * 512:(cb + 1) * 512], "s_wv")
                        E.wait(T, "s_wv")
                        for tb in range(4):
                            pb = f"ps{4 + (tb % 2)}"
                            E.acq(pb)
                            for i in range(16):
                                E.op(T, lambda t, i=i, tb=tb: t.matmul(
                                    ps[4 + (tb % 2)][:], XN3[i][:, tb * 128:(tb + 1) * 128],
                                    wvbv[:, i, :], start=(i == 0), stop=(i == 15)),
                                    "s_pe" if i == 15 else None)
                            E.wait(S, "s_pe")
                            E.drain(S)
                            E.op(S, lambda s, tb=tb, cb=cb: s.activation(
                                QTM4[tb][:, cb * 512:(cb + 1) * 512],
                                ps[4 + (tb % 2)][:], AF.Copy), "s_a")
                            E.rel(pb, "s_a")
                    for tb in range(4):
                        E.wait(V_, "s_a")
                        rope_norm(QTM4[tb], 16, col + tb * 128, cqsc)
                        E.wait(Y, "s_v")
                        E.dma(Y, D["qtmd"][(g - 2) * 512 + tb * 128:(g - 2) * 512 + (tb + 1) * 128, :],
                              QTM4[tb], "s_sp")

            if stop == "x3_pre":
                E.barrier()
                E.wait(Y, "s_sp")
                E.dma(Y, outy[:, :], D["x2d"][:, CH:2 * CH], "s_out")
                E.wait(Y, "s_out")
                return

            wbor = D["wboP"]
            for qb in range(2):
                E.barrier()
                E.wait(Y, "s_sp")
                E.dma(Y, arena[:, 52224:52224 + N_APPLY * 512].rearrange(
                    "p (a t) -> p a t", t=512),
                    D["bmask"][qb].rearrange("a p t -> p a t"), "s_x")
                for sub in range(4):
                    E.wait(Y, "s_a")
                    E.dma(Y, QTMR, D["qtmd"][qb * 512 + sub * 128: qb * 512 + (sub + 1) * 128, :], "s_qt")
                    E.wait(T, "s_qt")
                    for h in range(NH):
                        pq = f"ps{h % 2}"
                        E.acq(pq)
                        E.op(T, lambda t, h=h: t.matmul(
                            ps[h % 2][:, 0:128], QTMR[:, h * 128:(h + 1) * 128],
                            cidn[:], start=True, stop=True), "s_pe")
                        E.wait(S, "s_pe")
                        E.drain(S)
                        E.op(S, lambda s, h=h, sub=sub: s.activation(
                            QFMq[h][:, sub * 128:(sub + 1) * 128],
                            ps[h % 2][:, 0:128], AF.Copy), "s_a")
                        E.rel(pq, "s_a")
                E.wait(T, "s_a"); E.wait(V_, "s_x")

                for h in range(NH):
                    kvh = h // 4
                    apply_pos = {kb: i for i, kb in enumerate(_APPLY_KB[qb])}
                    E.acq("ps2"); E.acq("ps3")
                    exp_marks = [0, 0, 0, 0]
                    tiles = list(enumerate(_KTILES[qb]))
                    score_marks = {}

                    def emit_score(idx, kb):
                        rot = idx % 2
                        E.acq(f"ps{rot}")
                        E.op(T, lambda t, kb=kb, kvh=kvh, h=h, rot=rot: t.matmul(
                            ps[rot][:], KFM[kvh][:, kb * 128:(kb + 1) * 128],
                            QFMq[h], start=True, stop=True), "s_pe")
                        score_marks[idx] = E.cnt["s_pe"]

                    emit_score(0, tiles[0][1])
                    for idx, kb in tiles:
                        if idx + 1 < len(tiles):
                            emit_score(idx + 1, tiles[idx + 1][1])
                        rot = idx % 2
                        er = idx % 4
                        pq = f"ps{rot}"
                        E.wait(S, "s_pe", max(score_marks[idx], exp_marks[er]))
                        E.drain(S)
                        E.op(S, lambda s, rot=rot: s.activation(
                            ftmp[:, rot, :], ps[rot][:], AF.Tanh, scale=CAPS), "s_a")
                        E.rel(pq, "s_a")
                        E.drain(S)
                        E.op(S, lambda s, rot=rot, er=er: s.activation(
                            EXPB[er], ftmp[:, rot, :], AF.Exp, scale=50.0), "s_a")
                        exp_a = E.cnt["s_a"]
                        masked = kb in apply_pos
                        if masked:
                            E.wait(V_, "s_a", exp_a)
                            E.drain(V_)
                            E.op(V_, lambda v, er=er, kb=kb: v.tensor_mul(
                                EXPB[er], EXPB[er], BM[apply_pos[kb]]), "s_v")
                            E.wait(T, "s_v")
                        else:
                            E.wait(T, "s_a", exp_a)
                        first, last = idx == 0, idx == N_KT - 1
                        E.op(T, lambda t, er=er, first=first, last=last: t.matmul(
                            ps[3][0:1, :], conesb[:, 0:1], EXPB[er],
                            start=first, stop=last), None)
                        E.op(T, lambda t, er=er, kb=kb, kvh=kvh, first=first, last=last: t.matmul(
                            ps[2][:], VTM[kb][:, kvh * 128:(kvh + 1) * 128], EXPB[er],
                            start=first, stop=last),
                            "s_pe")
                        exp_marks[er] = E.cnt["s_pe"]
                    av_pe = E.cnt["s_pe"]
                    E.wait(V_, "s_pe", av_pe)
                    E.drain(V_)
                    E.op(V_, lambda v: v.reciprocal(inv1[:], ps[3][0:1, :]), "s_v")
                    E.rel("ps3", "s_v")
                    E.wait(T, "s_v")
                    E.acq("ps4")
                    E.op(T, lambda t: t.matmul(ps[4][:], cones1[:], inv1[:],
                                               start=True, stop=True), "s_pe")
                    E.wait(V_, "s_pe")
                    E.drain(V_)
                    E.op(V_, lambda v: v.tensor_copy(invr[:], ps[4][:]), "s_v")
                    E.rel("ps4", "s_v")
                    E.drain(V_)
                    E.op(V_, lambda v, h=h: v.tensor_mul(AN[h], ps[2][:], invr[:]), "s_v")
                    E.rel("ps2", "s_v")
                E.wait(T, "s_v")

                omarks = []
                E.wait(G, "s_pe")
                for oc in range(16):
                    rot = oc % 2
                    buf = wbufA[:, rot, :]
                    if oc >= 2:
                        E.wait(G, "s_pe", omarks[oc - 2])
                    E.dma(G, buf, wbor[:, oc * 2048:(oc + 1) * 2048], f"pG{rot}")
                    E.wait(T, f"pG{rot}")
                    pb = f"ps{4 + rot}"
                    E.acq(pb)
                    for i in range(16):
                        E.op(T, lambda t, i=i, rot=rot, buf=buf: t.matmul(
                            ps[4 + rot][:],
                            buf.rearrange("p (i o) -> p i o", o=128)[:, i, :],
                            AN[i], start=(i == 0), stop=(i == 15)),
                            "s_pe" if i == 15 else None)
                    omarks.append(E.cnt["s_pe"])
                    E.wait(Y, "s_v")
                    E.dma(Y, fx2[:, 0, :],
                          D["x2d"][oc * 128:(oc + 1) * 128, CH + qb * 512: CH + qb * 512 + 512],
                          "s_fx")
                    E.wait(V_, "s_pe")
                    E.wait(V_, "s_fx")
                    E.wait(V_, "s_sp")
                    E.drain(V_)
                    E.op(V_, lambda v, rot=rot: v.tensor_add(
                        ftmp[:, rot, :], ps[4 + rot][:], fx2[:, 0, :]), "s_v")
                    E.rel(pb, "s_v")
                    E.wait(Y, "s_v")
                    E.dma(Y, D["x3d"][oc * 128:(oc + 1) * 128, qb * 512: qb * 512 + 512],
                          ftmp[:, rot, :], "s_sp")

            if stop == "x3":
                E.barrier()
                E.wait(Y, "s_sp")
                E.dma(Y, outy[:, :], D["x3d"][:, :], "s_out")
                E.wait(Y, "s_out")
                return

            for g in range(CH // FG):
                col = g * FG
                E.barrier()
                E.wait(Y, "s_sp")
                E.dma(Y, xf32[:].rearrange("p (k t) -> p k t", t=512),
                      D["x3d"].rearrange("(k p) t -> p k t", p=128)[:, :, col:col + 512], "s_x")
                E.wait(V_, "s_x"); E.wait(S, "s_x"); E.wait(T, "s_x")
                ln_fm(XH, XNF, EPS_LN)
                E.wait(T, "s_v")
                ffn_sweep(XNF, D["x3d"], col, outy, col, "s_out")

            E.barrier()
            E.wait(Y, "s_out")

        def run_pass(who, eng):
            em = Emit(nc, who, eng, sems)
            program(em)
            if not first_counts:
                first_counts.update(em.cnt)
            else:
                assert em.cnt == first_counts, (who, em.cnt, first_counts)

        @block.sync
        def _(e):
            run_pass("sync", e)

        @block.gpsimd
        def _(e):
            run_pass("gpsimd", e)

        @block.tensor
        def _(e):
            run_pass("tensor", e)

        @block.vector
        def _(e):
            run_pass("vector", e)

        @block.scalar
        def _(e):
            run_pass("scalar", e)

    return nc


def kernel(**inputs):
    maps = shard_inputs(inputs)
    nc = build_nc()
    res = run_bass_kernel_spmd(nc, maps, core_ids=list(range(NC)))
    out = np.zeros((BATCH, SEQ, DM), np.float32)
    for core in range(NC):
        b, c = divmod(core, 4)
        out[b, c * CH:(c + 1) * CH] = np.asarray(res.results[core]["outy"]).T
    return out
